# version 05
# baseline (speedup 1.0000x reference)
import os
import numpy as np
import ml_dtypes
import concourse.bass as bass
import concourse.mybir as mybir
from concourse.bass_utils import run_bass_kernel_spmd
from contextlib import ExitStack

F32 = mybir.dt.float32
BF16 = mybir.dt.bfloat16
I32 = mybir.dt.int32
AF = mybir.ActivationFunctionType
ALU = mybir.AluOpType
AX = mybir.AxisListType

NCORES = 8
B, S, D = 4, 2048, 4096
NT = B * S
TOWN = NT // NCORES
HS = 512
HD = 128
NHL = 4
H = 11008
QB = 256
NQB = S // QB
CB = 256
NCH = S // CB
SCALE = 1.0 / float(np.sqrt(128.0))
DMT = D // 128
HT = H // 128
UPR = H // NCORES
DNR = D // NCORES
DQF = 512


def nf4_terms():
    nf4 = np.array([-1.0, -0.6961928009986877, -0.5250730514526367,
                    -0.39491748809814453, -0.28444138169288635, -0.18477343022823334,
                    -0.09105003625154495, 0.0, 0.07958029955625534, 0.16093020141124725,
                    0.24611230194568634, 0.33791524171829224, 0.44070982933044434,
                    0.5626170039176941, 0.7229568362236023, 1.0], dtype=np.float64)
    alpha = nf4[0]
    beta = nf4[1] - nf4[0]
    terms = []
    for v in range(1, 15):
        g = (nf4[v + 1] - nf4[v]) - (nf4[v] - nf4[v - 1])
        if g >= 0:
            terms.append((g, -g * v, False))
        else:
            terms.append((g, -g * v, True))
            beta += g
            alpha += -g * v
    return float(alpha), float(beta), [(float(s), float(b), f) for s, b, f in terms]


def host_prep(inputs):
    x = np.asarray(inputs["L0_attn_x"])
    xr = np.ascontiguousarray(x.reshape(NT, D))
    wq_t = np.ascontiguousarray(np.asarray(inputs["L0_attn_Wq"]).T)
    wk_t = np.ascontiguousarray(np.asarray(inputs["L0_attn_Wk"]).T)
    wv_t = np.ascontiguousarray(np.asarray(inputs["L0_attn_Wv"]).T)
    wo_t = np.ascontiguousarray(np.asarray(inputs["L0_attn_Wo"]).T)

    inv = 1.0 / (10000.0 ** (np.arange(0, HD, 2, dtype=np.float64) / HD))
    t = np.arange(S, dtype=np.float64)
    fr = np.outer(t, inv)
    cos_t = np.concatenate([np.cos(fr)] * 2, axis=1).T
    sin_t = np.concatenate([np.sin(fr)] * 2, axis=1).T
    sin_eff = sin_t.copy()
    sin_eff[:64, :] *= -1.0
    cos_t = np.ascontiguousarray(cos_t.astype(np.float32))
    sin_eff = np.ascontiguousarray(sin_eff.astype(np.float32))

    kk = np.arange(128)[:, None]
    qq = np.arange(QB)[None, :]
    NEG = np.float32(-30000.0)
    m = np.zeros((128, 4, QB), np.float32)
    m[:, 0, :] = np.where(kk <= qq, 0.0, NEG)
    m[:, 1, :] = np.where(kk <= qq - 128, 0.0, NEG)
    m[:, 2, :] = np.where(kk > qq, 0.0, NEG)
    m[:, 3, :] = np.where(kk > qq - 128, 0.0, NEG)
    masks = np.ascontiguousarray(m.reshape(128, 4 * QB).astype(ml_dtypes.bfloat16))
    ident = np.ascontiguousarray(np.eye(128, dtype=ml_dtypes.bfloat16))

    _a, _b, terms = nf4_terms()
    consts = np.zeros((128, 16), np.float32)
    consts[:, 0] = 1e-5
    for i, (_s, bb, _f) in enumerate(terms):
        consts[:, 2 + i] = np.float32(bb)

    b_up = np.asarray(inputs["L0_ffn_b_up"]).astype(np.float32)
    bup_t = np.ascontiguousarray(b_up.reshape(HT, 128).T)
    bdn = np.ascontiguousarray(np.asarray(inputs["L0_ffn_b_down"]).astype(np.float32).reshape(1, D))
    aup = np.ascontiguousarray(np.asarray(inputs["L0_ffn_w_up_lora_a"]).astype(np.float32))
    bupl = np.ascontiguousarray(np.asarray(inputs["L0_ffn_w_up_lora_b"]).astype(np.float32))
    adn = np.ascontiguousarray(np.asarray(inputs["L0_ffn_w_down_lora_a"]).astype(np.float32))
    bdnl = np.ascontiguousarray(np.asarray(inputs["L0_ffn_w_down_lora_b"]).astype(np.float32))

    cu = np.asarray(inputs["L0_ffn_w_up_codes"]).reshape(H, D)
    au = np.asarray(inputs["L0_ffn_w_up_absmax"]).astype(np.float32).reshape(H, D // 64)
    cd = np.asarray(inputs["L0_ffn_w_down_codes"]).reshape(D, H)
    ad = np.asarray(inputs["L0_ffn_w_down_absmax"]).astype(np.float32).reshape(D, H // 64)

    in_maps = []
    for c in range(NCORES):
        in_maps.append(dict(
            x_own=np.ascontiguousarray(xr[c * TOWN:(c + 1) * TOWN]),
            wq_t=np.ascontiguousarray(wq_t[:, c * HS:(c + 1) * HS]),
            wk_t=np.ascontiguousarray(wk_t[:, c * HS:(c + 1) * HS]),
            wv_t=np.ascontiguousarray(wv_t[:, c * HS:(c + 1) * HS]),
            wo_t=wo_t,
            cos_t=cos_t, sin_eff=sin_eff, masks=masks, ident=ident, consts=consts,
            bup_t=bup_t, bdn=bdn, aup=aup, bupl=bupl, adn=adn, bdnl=bdnl,
            codes_up=np.ascontiguousarray(cu[c * UPR:(c + 1) * UPR]),
            am_up=np.ascontiguousarray(au[c * UPR:(c + 1) * UPR]),
            codes_dn=np.ascontiguousarray(cd[c * DNR:(c + 1) * DNR]),
            am_dn=np.ascontiguousarray(ad[c * DNR:(c + 1) * DNR]),
        ))
    return in_maps


def build(stage="full"):
    nbat = B
    if stage in ("qkv1", "attn1", "qp1", "qk1", "vmm1", "full1"):
        nbat = 1
        stage = stage[:-1]
    nc = bass.Bass("TRN2", target_bir_lowering=False, debug=False,
                   detect_race_conditions=not os.environ.get("KD_NORACE"))
    alpha, beta, terms = nf4_terms()

    x_own = nc.dram_tensor("x_own", [TOWN, D], F32, kind="ExternalInput").ap()
    wq_td = nc.dram_tensor("wq_t", [D, HS], F32, kind="ExternalInput").ap()
    wk_td = nc.dram_tensor("wk_t", [D, HS], F32, kind="ExternalInput").ap()
    wv_td = nc.dram_tensor("wv_t", [D, HS], F32, kind="ExternalInput").ap()
    wo_td = nc.dram_tensor("wo_t", [D, D], F32, kind="ExternalInput").ap()
    cos_td = nc.dram_tensor("cos_t", [HD, S], F32, kind="ExternalInput").ap()
    sin_td = nc.dram_tensor("sin_eff", [HD, S], F32, kind="ExternalInput").ap()
    masks_d = nc.dram_tensor("masks", [128, 4 * QB], BF16, kind="ExternalInput").ap()
    ident_d = nc.dram_tensor("ident", [128, 128], BF16, kind="ExternalInput").ap()
    consts_d = nc.dram_tensor("consts", [128, 16], F32, kind="ExternalInput").ap()
    bup_td = nc.dram_tensor("bup_t", [128, HT], F32, kind="ExternalInput").ap()
    bdn_d = nc.dram_tensor("bdn", [1, D], F32, kind="ExternalInput").ap()
    aup_d = nc.dram_tensor("aup", [D, 16], F32, kind="ExternalInput").ap()
    bupl_d = nc.dram_tensor("bupl", [16, H], F32, kind="ExternalInput").ap()
    adn_d = nc.dram_tensor("adn", [H, 16], F32, kind="ExternalInput").ap()
    bdnl_d = nc.dram_tensor("bdnl", [16, D], F32, kind="ExternalInput").ap()
    codes_up_d = nc.dram_tensor("codes_up", [UPR, D], I32, kind="ExternalInput").ap()
    am_up_d = nc.dram_tensor("am_up", [UPR, D // 64], F32, kind="ExternalInput").ap()
    codes_dn_d = nc.dram_tensor("codes_dn", [DNR, H], I32, kind="ExternalInput").ap()
    am_dn_d = nc.dram_tensor("am_dn", [DNR, H // 64], F32, kind="ExternalInput").ap()

    out_own = nc.dram_tensor("out_own", [TOWN, D], F32, kind="ExternalOutput").ap()

    xn_in = nc.dram_tensor("xn_in", [TOWN, D], BF16, kind="Internal").ap()
    xn_ag = [nc.dram_tensor(f"xn_ag{k}", [NCORES * 256, D], BF16, kind="Internal").ap()
             for k in range(4)]
    o_in = nc.dram_tensor("o_in", [B * HS, S], BF16, kind="Internal").ap()
    ot_all = nc.dram_tensor("ot_all", [B * D, S], BF16, kind="Internal").ap()
    wup_in = nc.dram_tensor("wup_in", [UPR, D], BF16, kind="Internal").ap()
    wup_all = nc.dram_tensor("wup_all", [H, D], BF16, kind="Internal").ap()
    wdn_in = nc.dram_tensor("wdn_in", [DNR, H], BF16, kind="Internal").ap()
    wdn_all = nc.dram_tensor("wdn_all", [D, H], BF16, kind="Internal").ap()
    bupl_b = nc.dram_tensor("bupl_b", [16, H], BF16, kind="Internal").ap()
    bdnl_b = nc.dram_tensor("bdnl_b", [16, D], BF16, kind="Internal").ap()
    bdn_b = nc.dram_tensor("bdn_b", [1, D], BF16, kind="Internal").ap()

    dbg = {}
    if stage in ("qkv", "vmm"):
        dbg["q"] = nc.dram_tensor("dbg_q", [128, NHL * S], BF16, kind="ExternalOutput").ap()
        dbg["k"] = nc.dram_tensor("dbg_k", [128, NHL * S], BF16, kind="ExternalOutput").ap()
        dbg["v"] = nc.dram_tensor("dbg_v", [128, (S // 128) * HS], BF16, kind="ExternalOutput").ap()
        dbg["xn"] = nc.dram_tensor("dbg_xn", [TOWN, D], BF16, kind="ExternalOutput").ap()
    if stage == "attn":
        dbg["o"] = nc.dram_tensor("dbg_o", [B * HS, S], BF16, kind="ExternalOutput").ap()
    if stage in ("qp", "qk"):
        dbg["q"] = nc.dram_tensor("dbg_q", [128, NHL * S], BF16, kind="ExternalOutput").ap()
        if stage == "qk":
            dbg["k"] = nc.dram_tensor("dbg_k", [128, NHL * S], BF16, kind="ExternalOutput").ap()
    if stage == "ln":
        dbg["xn"] = nc.dram_tensor("dbg_xn", [TOWN, D], BF16, kind="ExternalOutput").ap()
        dbg["ag"] = nc.dram_tensor("dbg_ag", [NCORES * 256, D], BF16, kind="ExternalOutput").ap()
    if stage == "deq":
        dbg["wup"] = nc.dram_tensor("dbg_wup", [UPR, D], BF16, kind="ExternalOutput").ap()
        dbg["wdn"] = nc.dram_tensor("dbg_wdn", [DNR, H], BF16, kind="ExternalOutput").ap()

    es = ExitStack()
    with es:
        SBp = lambda nm, shp, dt=F32: es.enter_context(nc.sbuf_tensor(nm, shp, dt))
        SEM = lambda nm: es.enter_context(nc.semaphore(nm))

        consts_s = SBp("consts_s", [128, 16], F32)
        ones1 = SBp("ones1", [1, 128], BF16)
        ones1f = SBp("ones1f", [1, 128], F32)
        ones128 = SBp("ones128", [128, 1], BF16)

        s_misc = SEM("s_misc")
        s_miscg = SEM("s_miscg")
        s_amp = [SEM("s_amp0"), SEM("s_amp1")]
        s_x = SEM("s_x")
        s_xfree = SEM("s_xfree")
        s_xnst = SEM("s_xnst")
        s_cc = SEM("s_cc")
        s_xnc = SEM("s_xnc")
        s_xncf = SEM("s_xncf")
        s_w = SEM("s_w")
        s_pe = SEM("s_pe")
        s_ev = SEM("s_ev")
        s_ac = SEM("s_ac")
        s_ost = SEM("s_ost")
        s_cod = SEM("s_cod")
        s_codp = [SEM("s_codp0"), SEM("s_codp1")]
        s_xncp = [SEM("s_xncp0"), SEM("s_xncp1")]
        s_wp = [SEM("s_wp0"), SEM("s_wp1")]
        s_ostp = [SEM("s_ostp0"), SEM("s_ostp1")]
        s_dqf = SEM("s_dqf")
        s_wst = SEM("s_wst")
        s_f = SEM("s_f")
        s_ff = SEM("s_ff")
        s_out = SEM("s_out")
        s_dbg = SEM("s_dbg")

        blk = es.enter_context(nc.Block())
        c = dict(misc=0, miscg=0, x=0, xfree=0, xnst=0, cc=0, xnc=0, xncf=0, w=0, w0=0, w1=0,
                 pe=0, ev=0, ac=0, ost=0, ost0=0, ost1=0, cod=0, dqf=0, wst=0, f=0, ff=0, out=0)
        ms = {}

        def f(sy):
            sy.dma_start(out=consts_s[:], in_=consts_d[:]).then_inc(s_misc, 16)
            c["misc"] += 16
        blk.sync(f)
        MISC_ALL = c["misc"]

        def f(v):
            v.memset(ones1[:], 1.0)
            v.memset(ones1f[:], 1.0)
            v.memset(ones128[:], 1.0).then_inc(s_ev, 1)
            c["ev"] += 1
        blk.vector(f)
        ms["ones"] = c["ev"]

        F_CASTS = 0

        e1 = ExitStack()
        with e1:
            SB1 = lambda nm, shp, dt=F32: e1.enter_context(nc.sbuf_tensor(nm, shp, dt))
            masks_s = SB1("masks_s", [128, 4, QB], BF16)
            ident_s = SB1("ident_s", [128, 128], BF16)
            sqn = SB1("sqn", [128, S], BF16)
            pt = SB1("pt", [128, 10, QB], BF16)
            qn2 = SB1("qn2", [1, S], BF16)
            bndA = SB1("bndA", [1, S], BF16)
            bndB = SB1("bndB", [1, S], BF16)
            nb = SB1("nb", [1, S], BF16)
            kmax = SB1("kmax", [1, 1], F32)
            rden = SB1("rden", [1, QB], F32)
            otmp = SB1("otmp", [128, QB], BF16)
            ostg = [SB1(f"ostg{i}", [128, QB], BF16) for i in range(2)]
            rtmp1 = SB1("rtmp1", [128, CB], F32)
            rtmp2 = SB1("rtmp2", [128, CB], F32)
            stt = {k: SB1(f"st_{k}", [128, 1], F32) for k in
                   ["mur", "mu", "varr", "vart", "mu2", "var", "ln", "rstd", "negmu"]}
            wbuf0 = SB1("wbuf0", [128, DMT, HS], BF16)

            eln = ExitStack()
            with eln:
                xt = eln.enter_context(nc.sbuf_tensor("xt", [128, D], F32))
                st_sq = eln.enter_context(nc.sbuf_tensor("st_sq", [128, D], BF16))
                xnout = eln.enter_context(nc.sbuf_tensor("xnout", [128, D], BF16))

                for i in range(TOWN // 128):
                    def f(sy, i=i):
                        if i > 0:
                            sy.wait_ge(s_xfree, i)
                        sy.dma_start(out=xt[:], in_=x_own[i * 128:(i + 1) * 128, :]).then_inc(s_x, 16)
                        c["x"] += 16
                    blk.sync(f)
                    X_T = c["x"]

                    def f(v, X_T=X_T, i=i):
                        v.wait_ge(s_misc, MISC_ALL)
                        v.wait_ge(s_x, X_T)
                        v.tensor_reduce(stt["mur"][:], xt[:], axis=AX.X, op=ALU.add).then_inc(s_ev, 1)
                        c["ev"] += 1
                        v.wait_ge(s_ev, c["ev"])
                        v.tensor_scalar(stt["mu"][:], stt["mur"][:], 1.0 / D, 0.0, ALU.mult, ALU.add).then_inc(s_ev, 1)
                        c["ev"] += 1
                    blk.vector(f)
                    EV_MU = c["ev"]

                    def f(s, EV_MU=EV_MU):
                        s.wait_ge(s_ev, EV_MU)
                        s.activation(st_sq[:], xt[:], AF.Square, bias=consts_s[:, 1:2], scale=1.0).then_inc(s_ac, 1)
                        c["ac"] += 1
                    blk.scalar(f)
                    AC_SQ = c["ac"]

                    def f(v, AC_SQ=AC_SQ):
                        v.wait_ge(s_ac, AC_SQ)
                        v.tensor_reduce(stt["varr"][:], st_sq[:], axis=AX.X, op=ALU.add).then_inc(s_ev, 1)
                        c["ev"] += 1
                        v.wait_ge(s_ev, c["ev"])
                        v.tensor_scalar(stt["vart"][:], stt["varr"][:], 1.0 / D, 0.0, ALU.mult, ALU.add).then_inc(s_ev, 1)
                        c["ev"] += 1
                        v.tensor_tensor(stt["mu2"][:], stt["mu"][:], stt["mu"][:], ALU.mult).then_inc(s_ev, 1)
                        c["ev"] += 1
                        v.wait_ge(s_ev, c["ev"])
                        v.tensor_tensor(stt["var"][:], stt["vart"][:], stt["mu2"][:], ALU.subtract).then_inc(s_ev, 1)
                        c["ev"] += 1
                    blk.vector(f)
                    EV_VAR = c["ev"]

                    def f(s, EV_VAR=EV_VAR):
                        s.wait_ge(s_ev, EV_VAR)
                        s.activation(stt["ln"][:], stt["var"][:], AF.Ln, bias=consts_s[:, 0:1], scale=1.0).then_inc(s_ac, 1)
                        c["ac"] += 1
                        s.wait_ge(s_ac, c["ac"])
                        s.activation(stt["rstd"][:], stt["ln"][:], AF.Exp, bias=consts_s[:, 1:2], scale=-0.5).then_inc(s_ac, 1)
                        c["ac"] += 1
                    blk.scalar(f)
                    AC_RSTD = c["ac"]

                    def f(v, AC_RSTD=AC_RSTD, i=i):
                        v.wait_ge(s_ac, AC_RSTD)
                        if i > 0:
                            v.wait_ge(s_xnst, 16 * i)
                        v.tensor_scalar(stt["negmu"][:], stt["mu"][:], -1.0, 0.0, ALU.mult, ALU.add).then_inc(s_ev, 1)
                        c["ev"] += 1
                        v.wait_ge(s_ev, c["ev"])
                        v.tensor_scalar(xnout[:], xt[:], stt["negmu"][:], stt["rstd"][:], ALU.add, ALU.mult).then_inc(s_ev, 1)
                        c["ev"] += 1
                        c["xfree"] += 1
                        v.nop().then_inc(s_xfree, 1)
                    blk.vector(f)
                    EV_XN = c["ev"]

                    def f(sy, i=i, EV_XN=EV_XN):
                        sy.wait_ge(s_ev, EV_XN)
                        sy.dma_start(out=xn_in[i * 128:(i + 1) * 128, :], in_=xnout[:]).then_inc(s_xnst, 16)
                        c["xnst"] += 16
                    blk.sync(f)
            LN_ALL = c["xnst"]

            vs = SB1("vs", [128, S // 128, HS], BF16)
            wbuf1 = SB1("wbuf1", [128, DMT, HS], BF16)
            wbuf = [wbuf0, wbuf1]
            xnb = [SB1(f"xnb{i}", [128, DMT, CB], BF16) for i in range(2)]
            qs = SB1("qs", [128, NHL, S], BF16)
            ks = SB1("ks", [128, NHL, S], BF16)
            cos_s = SB1("cos_s", [HD, S], F32)
            sin_s = SB1("sin_s", [HD, S], F32)
            cfb = [SB1(f"cf{i}", [128, DQF], F32) for i in range(2)]
            amb = [SB1(f"am{i}", [128, DQF // 64], F32) for i in range(2)]
            dq_tmp = [SB1(f"dqt{i}", [128, DQF], BF16) for i in range(4)]
            dq_acc = SB1("dq_acc", [128, DQF], F32)
            dq_w = SB1("dq_w", [128, DQF], BF16)

            def f(sy):
                sy.dma_start(out=cos_s[:], in_=cos_td[:]).then_inc(s_misc, 16)
                sy.dma_start(out=sin_s[:], in_=sin_td[:]).then_inc(s_misc, 16)
                sy.dma_start(out=masks_s[:], in_=masks_d.rearrange("p (a q) -> p a q", q=QB)).then_inc(s_misc, 16)
                sy.dma_start(out=ident_s[:], in_=ident_d[:]).then_inc(s_misc, 16)
                c["misc"] += 64
            blk.sync(f)
            MISC2 = c["misc"]

            if stage != "ln":
                def f(g):
                    g.dma_start(out=wbuf[0][:],
                                in_=wq_td.rearrange("(a p) q -> p a q", p=128)).then_inc(s_wp[0], 16)
                    c["w0"] += 16
                blk.gpsimd(f)
                ms["WQ0"] = (s_wp[0], c["w0"])

            def f(g):
                for k in range(4):
                    g.wait_ge(s_xnst, (2 * k + 2) * 16)
                    g.collective_compute("AllGather", ALU.bypass, replica_groups=[list(range(NCORES))],
                                         ins=[xn_in[256 * k:256 * (k + 1), :].opt()],
                                         outs=[xn_ag[k].opt()]).then_inc(s_cc, 1)
                    c["cc"] += 1
            blk.gpsimd(f)

            if stage != "ln":
                def f(g):
                    g.wait_ge(s_xnst, LN_ALL)
                    g.dma_start(out=wbuf[1][:],
                                in_=wk_td.rearrange("(a p) q -> p a q", p=128)).then_inc(s_wp[1], 16)
                    c["w1"] += 16
                blk.gpsimd(f)
                ms["WK0"] = (s_wp[1], c["w1"])

            def xn_src(t0):
                r = t0 // TOWN
                k = (t0 % TOWN) // 256
                row0 = 256 * r + (t0 % 256)
                return xn_ag[k][row0:row0 + CB, :]

            up_sched = []
            for r0 in range(0, UPR, 128):
                nr = min(128, UPR - r0)
                for col0 in range(0, D, DQF):
                    up_sched.append((codes_up_d, am_up_d, r0, nr, col0, DQF, wup_in))
            dn_sched = []
            for r0 in range(0, DNR, 128):
                for col0 in range(0, H, DQF):
                    ncol = min(DQF, H - col0)
                    dn_sched.append((codes_dn_d, am_dn_d, r0, 128, col0, ncol, wdn_in))
            dq_all = (up_sched + dn_sched) if stage in ("full", "deq", "p1", "fu", "fd", "fx", "fuu", "fz") else []
            N_UP = len(up_sched)
            N_DNC = (H + DQF - 1) // DQF
            N_DQ = len(dq_all)

            cc_items = []
            if stage in ("full", "deq", "p1", "fu", "fd", "fx", "fuu", "fz"):
                def mkup():
                    def f(g):
                        g.wait_ge(s_wst, 16 * N_UP)
                        g.collective_compute("AllGather", ALU.bypass,
                                             replica_groups=[list(range(NCORES))],
                                             ins=[wup_in[:, :].opt()],
                                             outs=[wup_all[:, :].opt()]).then_inc(s_cc, 1)
                        c["cc"] += 1
                    return f
                cc_items.append((N_UP, mkup()))
            if stage in ("attn", "full", "p1", "fu", "fd", "fx", "fuu", "fz"):
                for b in range(nbat):
                    def mk(b=b):
                        def f(g):
                            g.wait_ge(s_ostp[0], 16 * 16 * (b + 1))
                            g.wait_ge(s_ostp[1], 16 * 16 * (b + 1))
                            g.collective_compute("AllGather", ALU.bypass,
                                                 replica_groups=[list(range(NCORES))],
                                                 ins=[o_in[HS * b:HS * (b + 1), :].opt()],
                                                 outs=[ot_all[b * D:(b + 1) * D, :].opt()]).then_inc(s_cc, 1)
                            c["cc"] += 1
                        return f
                    cc_items.append((44 * (b + 1) if stage in ("full", "p1", "fu", "fd", "fx", "fuu", "fz") else 0, mk()))
            if stage in ("full", "deq", "p1", "fu", "fd", "fx", "fuu", "fz"):
                def mkdn():
                    def f(g):
                        g.wait_ge(s_wst, 16 * N_DQ)
                        g.collective_compute("AllGather", ALU.bypass,
                                             replica_groups=[list(range(NCORES))],
                                             ins=[wdn_in[:, :].opt()],
                                             outs=[wdn_all[:, :].opt()]).then_inc(s_cc, 1)
                        c["cc"] += 1
                    return f
                cc_items.append((N_DQ, mkdn()))
            cc_items.sort(key=lambda kv: kv[0])
            cc_ptr = [0]
            dq_state = dict(chunk=0)

            def flush_cc():
                while cc_ptr[0] < len(cc_items) and cc_items[cc_ptr[0]][0] <= dq_state["chunk"]:
                    blk.gpsimd(cc_items[cc_ptr[0]][1])
                    cc_ptr[0] += 1

            def emit_dequant_chunk(codes_src, am_src, r0, nr, col0, ncol, dst):
                ci = dq_state["chunk"]
                bi = ci % 2

                def f(g, bi=bi, ci=ci, r0=r0, nr=nr, col0=col0, ncol=ncol,
                      codes_src=codes_src):
                    if ci >= 2:
                        g.wait_ge(s_ac, ms[f"dqterm_{ci - 2}"])
                        g.wait_ge(s_ev, ms[f"dqbase_{ci - 2}"])
                    g.dma_start(out=cfb[bi][0:nr, 0:ncol],
                                in_=codes_src[r0:r0 + nr, col0:col0 + ncol]).then_inc(s_codp[bi], 16)
                    c["cod"] += 16
                blk.gpsimd(f)

                def f(sy, bi=bi, ci=ci, r0=r0, nr=nr, col0=col0, ncol=ncol, am_src=am_src):
                    if ci >= 2:
                        sy.wait_ge(s_ev, ms[f"dqw_{ci - 2}"])
                    sy.dma_start(out=amb[bi][0:nr, 0:ncol // 64],
                                 in_=am_src[r0:r0 + nr, col0 // 64:(col0 + ncol) // 64]).then_inc(s_amp[bi], 16)
                    c["cod"] += 16
                blk.sync(f)
                COD_T = (s_codp[bi], 16 * (ci // 2 + 1))
                AM_T = (s_amp[bi], 16 * (ci // 2 + 1))

                def f(v, bi=bi, COD_T=COD_T, AM_T=AM_T, nr=nr, ncol=ncol):
                    v.wait_ge(*COD_T)
                    v.wait_ge(*AM_T)
                    v.wait_ge(s_ev, c["ev"])
                    v.tensor_scalar(dq_acc[0:nr, 0:ncol], cfb[bi][0:nr, 0:ncol],
                                    beta, alpha, ALU.mult, ALU.add).then_inc(s_ev, 1)
                    c["ev"] += 1
                blk.vector(f)
                EV_BASE = c["ev"]
                ms[f"dqbase_{ci}"] = EV_BASE

                AC_BASE = c["ac"]
                tmp_war = tuple(ms.get(f"dqtmp{k}") for k in range(4))

                for lo, hi in ((0, 7), (7, len(terms))):
                    def f(s, COD_T=COD_T, AM_T=AM_T, bi=bi, nr=nr, ncol=ncol,
                          AC_BASE=AC_BASE, EV_BASE=EV_BASE, tmp_war=tmp_war,
                          lo=lo, hi=hi):
                        s.wait_ge(*COD_T)
                        s.wait_ge(*AM_T)
                        for ti in range(lo, hi):
                            sc_ = terms[ti][0]
                            tb = ti % 4
                            if ti < 4:
                                if tmp_war[tb] is not None:
                                    s.wait_ge(s_ev, tmp_war[tb])
                            else:
                                s.wait_ge(s_ev, EV_BASE + ti - 3)
                            s.activation(dq_tmp[tb][0:nr, 0:ncol], cfb[bi][0:nr, 0:ncol],
                                         AF.Relu, bias=consts_s[0:nr, 2 + ti:3 + ti],
                                         scale=sc_).then_inc(s_ac, 1)
                    blk.scalar(f)
                c["ac"] += len(terms)
                ms[f"dqterm_{ci}"] = c["ac"]

                def f(v, bi=bi, nr=nr, ncol=ncol, ci=ci, AC_BASE=AC_BASE, EV_BASE=EV_BASE):
                    for ti, (_sc, _bias, sub) in enumerate(terms):
                        v.wait_ge(s_ac, AC_BASE + ti + 1)
                        v.wait_ge(s_ev, EV_BASE + ti)
                        v.tensor_tensor(dq_acc[0:nr, 0:ncol], dq_acc[0:nr, 0:ncol],
                                        dq_tmp[ti % 4][0:nr, 0:ncol],
                                        ALU.subtract if sub else ALU.add).then_inc(s_ev, 1)
                    v.wait_ge(s_wst, 16 * ci)
                    v.wait_ge(s_ev, EV_BASE + len(terms))
                    insts = []
                    for b64 in range(ncol // 64):
                        insts.append(v.tensor_scalar(dq_w[0:nr, 64 * b64:64 * (b64 + 1)],
                                                     dq_acc[0:nr, 64 * b64:64 * (b64 + 1)],
                                                     amb[bi][0:nr, b64:b64 + 1], None, ALU.mult))
                    insts[-1].then_inc(s_ev, 1)
                blk.vector(f)
                c["ev"] += len(terms) + 1
                for k in range(4):
                    ti_last = max(t for t in range(len(terms)) if t % 4 == k)
                    ms[f"dqtmp{k}"] = EV_BASE + ti_last + 1
                EV_W = c["ev"]
                ms[f"dqw_{ci}"] = EV_W

                def f(sy, r0=r0, nr=nr, col0=col0, ncol=ncol, dst=dst, EV_W=EV_W):
                    sy.wait_ge(s_ev, EV_W)
                    sy.dma_start(out=dst[r0:r0 + nr, col0:col0 + ncol],
                                 in_=dq_w[0:nr, 0:ncol]).then_inc(s_wst, 16)
                    c["wst"] += 16
                blk.sync(f)
                dq_state["chunk"] += 1
                flush_cc()

            dq_ptr = [0]

            def pump_dequant(n):
                for _ in range(n):
                    if dq_ptr[0] < len(dq_all):
                        emit_dequant_chunk(*dq_all[dq_ptr[0]])
                        dq_ptr[0] += 1

            chunk_pe = {}
            if stage == "ln":
                def f(sy):
                    sy.wait_ge(s_cc, 4)
                    sy.wait_ge(s_xnst, c["xnst"])
                    sy.dma_start(out=dbg["xn"], in_=xn_in[:]).then_inc(s_dbg, 16)
                    sy.dma_start(out=dbg["ag"], in_=xn_ag[3][:, :]).then_inc(s_dbg, 16)
                    sy.wait_ge(s_dbg, 32)
                blk.sync(f)

            if stage != "ln":
                wsrc = {0: wq_td, 1: wk_td, 2: wv_td}

                def emit_wload(mat, slot, waits):
                    def f(g, mat=mat, slot=slot, waits=tuple(waits)):
                        for (sem, val) in waits:
                            g.wait_ge(sem, val)
                        g.dma_start(out=wbuf[slot][:],
                                    in_=wsrc[mat].rearrange("(a p) q -> p a q", p=128)).then_inc(s_wp[slot], 16)
                        c[f"w{slot}"] += 16
                    blk.gpsimd(f)
                    return (s_wp[slot], c[f"w{slot}"])

                for b in range(nbat):
                    epsum = ExitStack()
                    psA = epsum.enter_context(nc.psum_tensor(f"psA{b}", [128, 4, 512], F32))
                    psB = epsum.enter_context(nc.psum_tensor(f"psB{b}", [128, 4, 512], F32))
                    slotQ, slotK = b % 2, (b + 1) % 2
                    if b == 0:
                        WQ_T = ms["WQ0"]
                        WK_T = ms["WK0"]
                    else:
                        WQ_T = emit_wload(0, slotQ, [(s_pe, ms[f"vpass_{b-1}"])])
                        WK_T = emit_wload(1, slotK, [(s_pe, ms[f"kpass_{b-1}"])])

                    mats = (((WQ_T, slotQ, qs, "qpass"),) if stage == "qp" else
                            ((WQ_T, slotQ, qs, "qpass"), (WK_T, slotK, ks, "kpass")))
                    if stage == "qp":
                        pass
                    for mi, (W_T, slot, dst, mname) in enumerate(mats):
                        for cch in range(NCH):
                            gg = (b * 3 + mi) * NCH + cch
                            bi = gg % 2
                            ps = psA if cch % 2 == 0 else psB

                            def f(sy, b=b, cch=cch, gg=gg, bi=bi):
                                sy.wait_ge(s_cc, ((S * b + CB * cch) % TOWN) // 256 + 1)
                                if gg >= 2:
                                    sy.wait_ge(s_pe, chunk_pe[gg - 2])
                                sy.dma_start(out=xnb[bi][:],
                                             in_=xn_src(S * b + CB * cch),
                                             transpose=True).then_inc(s_xncp[bi], 16)
                                c["xnc"] += 16
                            blk.sync(f)
                            XNC_T = (s_xncp[bi], 16 * (gg // 2 + 1))

                            def f(t, W_T=W_T, XNC_T=XNC_T, slot=slot, bi=bi, ps=ps, cch=cch,
                                  aw=("exp" in ms and (ms["exp"], ms["otmp"], ms["rden"], ms["ofin"])) or None):
                                t.wait_ge(*W_T)
                                t.wait_ge(*XNC_T)
                                if aw is not None:
                                    t.wait_ge(s_ac, aw[0])
                                    t.wait_ge(s_ev, aw[1])
                                    t.wait_ge(s_ev, aw[2])
                                    t.wait_ge(s_ev, aw[3])
                                if "vevac_s0" in ms:
                                    t.wait_ge(s_ac, ms["vevac_s0"])
                                    t.wait_ge(s_ac, ms["vevac_s1"])
                                if f"rope_p{cch % 2}" in ms:
                                    t.wait_ge(s_ev, ms[f"rope_p{cch % 2}"])
                                for dq in range(4):
                                    for dm in range(DMT):
                                        inst = t.matmul(ps[:, dq, 0:QB], wbuf[slot][:, dm, dq * 128:(dq + 1) * 128],
                                                        xnb[bi][:, dm, :], start=(dm == 0),
                                                        stop=(dm == DMT - 1))
                                inst.then_inc(s_pe, 1)
                                c["pe"] += 1
                            blk.tensor(f)
                            PE_P = c["pe"]
                            chunk_pe[gg] = PE_P

                            def f(v, cch=cch, PE_P=PE_P, ps=ps, dst=dst):
                                v.wait_ge(s_pe, PE_P)
                                v.wait_ge(s_misc, MISC2)
                                tsl = slice(CB * cch, CB * (cch + 1))
                                for dq in range(4):
                                    v.wait_ge(s_ev, c["ev"])
                                    v.tensor_tensor(rtmp1[:], ps[:, dq, 0:QB], cos_s[:, tsl], ALU.mult)
                                    v.tensor_copy(rtmp2[0:64, :], ps[:, dq, 0:QB][64:128, :])
                                    v.tensor_copy(rtmp2[64:128, :], ps[:, dq, 0:QB][0:64, :]).then_inc(s_ev, 1)
                                    c["ev"] += 1
                                    v.wait_ge(s_ev, c["ev"])
                                    v.tensor_tensor(rtmp2[:], rtmp2[:], sin_s[:, tsl], ALU.mult).then_inc(s_ev, 1)
                                    c["ev"] += 1
                                    v.wait_ge(s_ev, c["ev"])
                                    v.tensor_tensor(dst[:, dq, tsl], rtmp1[:], rtmp2[:],
                                                    ALU.add).then_inc(s_ev, 1)
                                    c["ev"] += 1
                                ms[f"rope_p{cch % 2}"] = c["ev"]
                            blk.vector(f)
                        ms[f"{mname}_{b}"] = c["pe"]
                        if mname == "kpass" and stage not in ("qp", "qk"):
                            ms["WV_T"] = emit_wload(2, slotQ, [(s_pe, ms[f"qpass_{b}"])])
                        pump_dequant(4)

                    if stage in ("qp", "qk"):
                        def f(sy):
                            sy.wait_ge(s_ev, ms["rope_p0"])
                            sy.wait_ge(s_ev, ms["rope_p1"])
                            sy.dma_start(out=dbg["q"], in_=qs[:].rearrange("p a t -> p (a t)")).then_inc(s_dbg, 16)
                            if stage == "qk":
                                sy.dma_start(out=dbg["k"], in_=ks[:].rearrange("p a t -> p (a t)")).then_inc(s_dbg, 16)
                            sy.wait_ge(s_dbg, 32 if stage == "qk" else 16)
                        blk.sync(f)
                        epsum.close()
                        continue
                    WV_T = ms["WV_T"]
                    epsum.close()
                    epsum = ExitStack()
                    psV = epsum.enter_context(nc.psum_tensor(f"psV{b}", [128, 2, HS], F32))

                    for cch in range(NCH):
                        gg = (b * 3 + 2) * NCH + cch
                        bi = gg % 2

                        def f(sy, b=b, cch=cch, gg=gg, bi=bi):
                            sy.wait_ge(s_cc, ((S * b + CB * cch) % TOWN) // 256 + 1)
                            if gg >= 2:
                                sy.wait_ge(s_pe, chunk_pe[gg - 2])
                            sy.dma_start(out=xnb[bi][:],
                                         in_=xn_src(S * b + CB * cch),
                                         transpose=True).then_inc(s_xncp[bi], 16)
                            c["xnc"] += 16
                        blk.sync(f)
                        XNC_T = (s_xncp[bi], 16 * (gg // 2 + 1))

                        for ts_ in range(2):
                            def f(t, WV_T=WV_T, XNC_T=XNC_T, slotV=slotQ, bi=bi, ts_=ts_,
                                  rp=(ms.get("rope_p0"), ms.get("rope_p1")),
                                  aw=("exp" in ms and (ms["exp"], ms["otmp"], ms["rden"], ms["ofin"])) or None):
                                t.wait_ge(*WV_T)
                                t.wait_ge(*XNC_T)
                                if rp[0]:
                                    t.wait_ge(s_ev, rp[0])
                                if rp[1]:
                                    t.wait_ge(s_ev, rp[1])
                                if aw is not None:
                                    t.wait_ge(s_ac, aw[0])
                                    t.wait_ge(s_ev, aw[1])
                                    t.wait_ge(s_ev, aw[2])
                                    t.wait_ge(s_ev, aw[3])
                                if f"vevacv_s{ts_}" in ms:
                                    t.wait_ge(s_ev, ms[f"vevacv_s{ts_}"])
                                for dm in range(DMT):
                                    inst = t.matmul(psV[:, ts_, :],
                                                    xnb[bi][:, dm, ts_ * 128:(ts_ + 1) * 128],
                                                    wbuf[slotV][:, dm, :], start=(dm == 0), stop=(dm == DMT - 1))
                                inst.then_inc(s_pe, 1)
                                c["pe"] += 1
                                if ts_ == 1:
                                    chunk_pe[gg] = c["pe"]
                            blk.tensor(f)
                            PE_VS = c["pe"]

                            def f(s, cch=cch, ts_=ts_, PE_VS=PE_VS):
                                s.wait_ge(s_pe, PE_VS)
                                s.activation(vs[:, 2 * cch + ts_, :], psV[:, ts_, :], AF.Relu,
                                             bias=consts_s[:, 1:2], scale=1.0).then_inc(s_ac, 1) if False else None
                                s.nop().then_inc(s_ac, 1)
                                c["ac"] += 1
                                ms[f"vevac_s{ts_}"] = c["ac"]
                            blk.scalar(f)

                            def f(v, cch=cch, ts_=ts_, PE_VS=PE_VS):
                                v.wait_ge(s_pe, PE_VS)
                                if stage != "vmm":
                                    v.tensor_copy(vs[:, 2 * cch + ts_, :], psV[:, ts_, :]).then_inc(s_ev, 1)
                                else:
                                    v.nop().then_inc(s_ev, 1)
                                c["ev"] += 1
                                ms[f"vevacv_s{ts_}"] = c["ev"]
                            blk.vector(f)
                    ms[f"vpass_{b}"] = c["pe"]
                    epsum.close()
                    pump_dequant(4)

                    if stage in ("qkv", "deq"):
                        continue
                    for h in range(NHL):
                        with (
                            nc.psum_tensor(f"qn2p_{b}_{h}", [1, S], F32) as qn2p,
                            nc.psum_tensor(f"kn2p_{b}_{h}", [1, S], F32) as kn2p,
                        ):
                            def f(s, h=h):
                                s.wait_ge(s_ac, ms["vevac_s0"])
                                s.wait_ge(s_ac, ms["vevac_s1"])
                                s.wait_ge(s_ev, ms["rope_p0"])
                                s.wait_ge(s_ev, ms["rope_p1"])
                                if "pe_kn" in ms:
                                    s.wait_ge(s_pe, ms["pe_kn"])
                                s.activation(sqn[:], qs[:, h, :], AF.Square, bias=consts_s[:, 1:2],
                                             scale=1.0).then_inc(s_ac, 1)
                                c["ac"] += 1
                            blk.scalar(f)
                            AC_SQQ = c["ac"]

                            def f(t, AC_SQQ=AC_SQQ, qn2p=qn2p):
                                t.wait_ge(s_ac, AC_SQQ)
                                t.wait_ge(s_ev, ms["ones"])
                                if "exp" in ms:
                                    t.wait_ge(s_ac, ms["exp"])
                                    t.wait_ge(s_ev, ms["otmp"])
                                    t.wait_ge(s_ev, ms["rden"])
                                    t.wait_ge(s_ev, ms["ofin"])
                                for ch in range(4):
                                    inst = t.matmul(qn2p[0:1, 512 * ch:512 * (ch + 1)], ones128[:],
                                                    sqn[:, 512 * ch:512 * (ch + 1)], start=True, stop=True)
                                inst.then_inc(s_pe, 1)
                                c["pe"] += 1
                            blk.tensor(f)
                            PE_QN = c["pe"]

                            def f(s, h=h, PE_QN=PE_QN):
                                s.wait_ge(s_pe, PE_QN)
                                s.activation(sqn[:], ks[:, h, :], AF.Square, bias=consts_s[:, 1:2],
                                             scale=1.0).then_inc(s_ac, 1)
                                c["ac"] += 1
                            blk.scalar(f)
                            AC_SQK = c["ac"]

                            def f(t, AC_SQK=AC_SQK, kn2p=kn2p):
                                t.wait_ge(s_ac, AC_SQK)
                                for ch in range(4):
                                    inst = t.matmul(kn2p[0:1, 512 * ch:512 * (ch + 1)], ones128[:],
                                                    sqn[:, 512 * ch:512 * (ch + 1)], start=True, stop=True)
                                inst.then_inc(s_pe, 1)
                                c["pe"] += 1
                            blk.tensor(f)
                            PE_KN = c["pe"]
                            ms["pe_kn"] = PE_KN

                            def f(v, PE_KN=PE_KN, qn2p=qn2p, kn2p=kn2p):
                                v.wait_ge(s_pe, PE_KN)
                                v.tensor_reduce(kmax[:], kn2p[:], axis=AX.X, op=ALU.max).then_inc(s_ev, 1)
                                c["ev"] += 1
                                v.tensor_copy(qn2[:], qn2p[:]).then_inc(s_ev, 1)
                                c["ev"] += 1
                                v.wait_ge(s_ev, c["ev"])
                                v.tensor_scalar(bndA[0:1, :], qn2[0:1, :], kmax[0:1, 0:1], None,
                                                ALU.mult).then_inc(s_ev, 1)
                                c["ev"] += 1
                            blk.vector(f)
                            EV_BP = c["ev"]

                        def f(s, EV_BP=EV_BP):
                            s.wait_ge(s_ev, EV_BP)
                            s.activation(bndB[0:1, :], bndA[0:1, :], AF.Ln, bias=consts_s[0:1, 0:1],
                                         scale=1.0).then_inc(s_ac, 1)
                            c["ac"] += 1
                            s.wait_ge(s_ac, c["ac"])
                            s.activation(bndA[0:1, :], bndB[0:1, :], AF.Exp, bias=consts_s[0:1, 1:2],
                                         scale=0.5).then_inc(s_ac, 1)
                            c["ac"] += 1
                        blk.scalar(f)
                        AC_BND = c["ac"]

                        def f(v, AC_BND=AC_BND):
                            v.wait_ge(s_ac, AC_BND)
                            v.tensor_scalar(nb[0:1, :], bndA[0:1, :], -1.0, 0.0, ALU.mult,
                                            ALU.add).then_inc(s_ev, 1)
                            c["ev"] += 1
                        blk.vector(f)
                        EV_NB = c["ev"]

                        with (
                            nc.psum_tensor(f"sc_{b}_{h}", [128, 5, 512], F32) as sc,
                            nc.psum_tensor(f"ob_{b}_{h}", [128, QB], F32) as ob,
                            nc.psum_tensor(f"den_{b}_{h}", [1, QB], F32) as den,
                            nc.psum_tensor(f"bc_{b}_{h}", [128, QB], F32) as bc,
                        ):
                            for j in range(NQB):
                                kt = list(range(max(0, 2 * j - 8), 2 * j + 2))
                                nkt = len(kt)
                                waves = [kt[:5], kt[5:]]
                                PE0 = c["pe"]
                                AC0 = c["ac"]
                                nw = sum(1 for w in waves if w)

                                wn = 0
                                for wv, wkt in enumerate(waves):
                                    if not wkt:
                                        continue
                                    wn += 1

                                    def f(t, j=j, wv=wv, wkt=wkt, h=h, EV_NB=EV_NB, sc=sc,
                                          AC0=AC0):
                                        t.wait_ge(s_ev, EV_NB)
                                        t.wait_ge(s_misc, MISC2)
                                        if wv == 0:
                                            if f"exp_w" in ms:
                                                t.wait_ge(s_ac, ms["exp_w"])
                                            if "rden" in ms:
                                                t.wait_ge(s_ev, ms["rden"])
                                        else:
                                            t.wait_ge(s_ac, AC0 + 1)
                                        for idx, i in enumerate(wkt):
                                            t.matmul(sc[:, idx, 0:QB], ks[:, h, 128 * i:128 * (i + 1)],
                                                     qs[:, h, QB * j:QB * (j + 1)],
                                                     start=True, stop=False)
                                        for idx, i in enumerate(wkt):
                                            kind = None
                                            if i == 2 * j:
                                                kind = 0
                                            elif i == 2 * j + 1:
                                                kind = 1
                                            elif i == 2 * j - 8:
                                                kind = 2
                                            elif i == 2 * j - 7:
                                                kind = 3
                                            if kind is not None:
                                                t.matmul(sc[:, idx, 0:QB], ident_s[:],
                                                         masks_s[:, kind, :],
                                                         start=False, stop=False)
                                        for idx in range(len(wkt)):
                                            inst = t.matmul(sc[:, idx, 0:QB], ones1[:],
                                                            nb[0:1, QB * j:QB * (j + 1)],
                                                            start=False, stop=True)
                                        inst.then_inc(s_pe, 1)
                                        c["pe"] += 1
                                    blk.tensor(f)

                                def f(s, waves=waves, PE0=PE0, sc=sc):
                                    wn_ = 0
                                    for wv, wkt in enumerate(waves):
                                        if not wkt:
                                            continue
                                        wn_ += 1
                                        s.wait_ge(s_pe, PE0 + wn_)
                                        s.activation(pt[:, 5 * wv:5 * wv + len(wkt), :],
                                                     sc[:, 0:len(wkt), 0:QB], AF.Exp,
                                                     bias=consts_s[:, 1:2],
                                                     scale=SCALE).then_inc(s_ac, 1)
                                blk.scalar(f)
                                c["ac"] += nw
                                ms["exp_w"] = c["ac"]
                                ms["exp"] = c["ac"]
                                AC_EXP = c["ac"]

                                def f(t, kt=kt, nkt=nkt, h=h, AC_EXP=AC_EXP, ob=ob, den=den):
                                    t.wait_ge(s_ac, AC_EXP)
                                    if "otmp" in ms:
                                        t.wait_ge(s_ev, ms["otmp"])
                                    ptidx = [5 * (x // 5) + (x % 5) for x in range(nkt)]
                                    for n_, idx in enumerate(ptidx):
                                        t.matmul(den[0:1, :], ones128[:], pt[:, idx, :],
                                                 start=(n_ == 0), stop=(n_ == nkt - 1))
                                    for n_, i in enumerate(kt):
                                        inst = t.matmul(ob[:], vs[:, i, 128 * h:128 * (h + 1)],
                                                        pt[:, ptidx[n_], :],
                                                        start=(n_ == 0), stop=(n_ == nkt - 1))
                                    inst.then_inc(s_pe, 1)
                                    c["pe"] += 1
                                blk.tensor(f)
                                PE_AV = c["pe"]

                                def f(v, PE_AV=PE_AV, den=den):
                                    v.wait_ge(s_pe, PE_AV)
                                    v.wait_ge(s_ev, c["ev"])
                                    v.reciprocal(rden[0:1, :], den[0:1, :]).then_inc(s_ev, 1)
                                    c["ev"] += 1
                                    ms["rden"] = c["ev"]
                                blk.vector(f)
                                EV_RD = c["ev"]

                                def f(t, EV_RD=EV_RD, bc=bc):
                                    t.wait_ge(s_ev, EV_RD)
                                    t.matmul(bc[:], ones1f[:], rden[0:1, :], start=True,
                                             stop=True).then_inc(s_pe, 1)
                                    c["pe"] += 1
                                blk.tensor(f)
                                PE_BC = c["pe"]

                                def f(v, PE_BC=PE_BC, ob=ob):
                                    v.wait_ge(s_pe, PE_BC)
                                    v.wait_ge(s_ev, c["ev"])
                                    v.tensor_copy(otmp[:], ob[:]).then_inc(s_ev, 1)
                                    c["ev"] += 1
                                    ms["otmp"] = c["ev"]
                                blk.vector(f)
                                EV_OT = c["ev"]

                                ji = (h * NQB + j) % 2

                                def f(v, EV_OT=EV_OT, bc=bc, ji=ji,
                                      need=None):
                                    v.wait_ge(s_ev, EV_OT)
                                    if c[f"ost{ji}"] > 0:
                                        v.wait_ge(s_ostp[ji], c[f"ost{ji}"])
                                    v.tensor_tensor(ostg[ji][:], otmp[:], bc[:], ALU.mult).then_inc(s_ev, 1)
                                    c["ev"] += 1
                                    ms["ofin"] = c["ev"]
                                blk.vector(f)
                                EV_OF = c["ev"]

                                def f(sy, b=b, h=h, j=j, EV_OF=EV_OF, ji=ji):
                                    sy.wait_ge(s_ev, EV_OF)
                                    sy.dma_start(out=o_in[HS * b + 128 * h:HS * b + 128 * (h + 1),
                                                          QB * j:QB * (j + 1)],
                                                 in_=ostg[ji][:]).then_inc(s_ostp[ji], 16)
                                    c["ost"] += 16
                                    c[f"ost{ji}"] += 16
                                blk.sync(f)
                                pump_dequant(1)
                    ms[f"attn_{b}"] = c["pe"]

                pump_dequant(10000)
                flush_cc()
                while cc_ptr[0] < len(cc_items):
                    blk.gpsimd(cc_items[cc_ptr[0]][1])
                    cc_ptr[0] += 1

            WST_ALL = c["wst"]
            OST_ALL = c["ost"]
            CC_ALL = c["cc"]

            if stage in ("qkv", "vmm"):
                def f(sy):
                    sy.wait_ge(s_ev, ms["rope_p0"])
                    sy.wait_ge(s_ev, ms["rope_p1"])
                    sy.wait_ge(s_ac, ms["vevac_s0"])
                    sy.wait_ge(s_ac, ms["vevac_s1"])
                    sy.dma_start(out=dbg["q"], in_=qs[:].rearrange("p a t -> p (a t)")).then_inc(s_dbg, 16)
                    sy.dma_start(out=dbg["k"], in_=ks[:].rearrange("p a t -> p (a t)")).then_inc(s_dbg, 16)
                    sy.dma_start(out=dbg["v"], in_=vs[:].rearrange("p a t -> p (a t)")).then_inc(s_dbg, 16)
                    sy.wait_ge(s_xnst, c["xnst"])
                    sy.dma_start(out=dbg["xn"], in_=xn_in[:]).then_inc(s_dbg, 16)
                    sy.wait_ge(s_dbg, 64)
                blk.sync(f)
            if stage == "attn":
                def f(sy):
                    sy.wait_ge(s_ostp[0], c["ost0"])
                    sy.wait_ge(s_ostp[1], c["ost1"])
                    sy.dma_start(out=dbg["o"], in_=o_in[:]).then_inc(s_dbg, 16)
                    sy.wait_ge(s_dbg, 16)
                blk.sync(f)
            if stage in ("p1", "fz"):
                def f(sy):
                    sy.wait_ge(s_wst, WST_ALL)
                    sy.wait_ge(s_cc, c["cc"])
                    sy.wait_ge(s_ostp[0], c["ost0"])
                    sy.wait_ge(s_ostp[1], c["ost1"])
                blk.sync(f)
            if stage == "deq":
                def f(sy):
                    sy.wait_ge(s_wst, WST_ALL)
                    sy.dma_start(out=dbg["wup"], in_=wup_in[:]).then_inc(s_dbg, 16)
                    sy.dma_start(out=dbg["wdn"], in_=wdn_in[:]).then_inc(s_dbg, 16)
                    sy.wait_ge(s_dbg, 32)
                blk.sync(f)

        if stage in ("full", "fu", "fd", "fx", "fuu"):
            e2 = ExitStack()
            with e2:
                SB2 = lambda nm, shp, dt=BF16: e2.enter_context(nc.sbuf_tensor(nm, shp, dt))
                x2 = SB2("x2", [128, HT, 512])
                xo_sh = SB2("xo_sh", [128, DMT, 512])
                wu2 = [SB2(f"wu2_{i}", [128, DMT, 128]) for i in range(2)]
                wd2 = [SB2(f"wd2_{i}", [128, 4, 512]) for i in range(2)]
                wo2 = [SB2(f"wo2_{i}", [128, 8, 512]) for i in range(2)]
                aup2 = SB2("aup2", [128, DMT, 16])
                adn2 = SB2("adn2", [128, HT, 16])
                bupl2 = [SB2(f"bupl2_{i}", [16, 128]) for i in range(2)]
                bdnl2 = [SB2(f"bdnl2_{i}", [16, 512]) for i in range(2)]
                bdn2 = [SB2(f"bdn2_{i}", [1, 512]) for i in range(2)]
                bupt2 = e2.enter_context(nc.sbuf_tensor("bupt2", [128, HT], F32))
                u_s = SB2("u_s", [16, 512])
                ud_s = SB2("ud_s", [16, 512])
                xres = [e2.enter_context(nc.sbuf_tensor(f"xres{i}", [128, 512], F32)) for i in range(2)]
                outb = [e2.enter_context(nc.sbuf_tensor(f"outb{i}", [128, 512], F32)) for i in range(2)]

                s_fu = [SEM("s_fu0"), SEM("s_fu1")]
                s_fd = [SEM("s_fd0"), SEM("s_fd1")]
                s_fo = [SEM("s_fo0"), SEM("s_fo1")]
                s_fb = [SEM("s_fb0"), SEM("s_fb1")]
                s_fl = [SEM("s_fl0"), SEM("s_fl1")]
                s_fx = [SEM("s_fx0"), SEM("s_fx1")]
                s_xo = SEM("s_xo")
                s_outp = [SEM("s_outp0"), SEM("s_outp1")]
                cn = dict(fu=[0, 0], fd=[0, 0], fo=[0, 0], fb=[0, 0], fl=[0, 0], fx=[0, 0],
                          xo=0, outp=[0, 0])

                PH1 = [(s_ac, ms["exp"]), (s_ev, ms["otmp"]), (s_ev, ms["ofin"]),
                       (s_ev, ms["rden"]), (s_ev, ms["rope_p0"]), (s_ev, ms["rope_p1"]),
                       (s_ac, ms["vevac_s0"]), (s_ac, ms["vevac_s1"]),
                       (s_ev, ms[f"dqw_{N_DQ - 1}"] if N_DQ else 0),
                       (s_ac, ms[f"dqterm_{N_DQ - 1}"] if N_DQ else 0), (s_xnst, LN_ALL),
                       (s_ostp[0], c["ost0"]), (s_ostp[1], c["ost1"]),
                       (s_wst, c["wst"])]

                def ph1wait(eng):
                    for (sem, val) in PH1:
                        eng.wait_ge(sem, val)

                def f(sy):
                    ph1wait(sy)
                    sy.dma_start(out=bupt2[:], in_=bup_td[:]).then_inc(s_misc, 16)
                    c["misc"] += 16
                blk.sync(f)
                MISC3 = c["misc"]

                def f(g):
                    ph1wait(g)
                    g.dma_start(out=aup2[:], in_=aup_d.rearrange("(a p) r -> p a r", p=128)).then_inc(s_miscg, 16)
                    g.dma_start(out=adn2[:], in_=adn_d.rearrange("(a p) r -> p a r", p=128)).then_inc(s_miscg, 16)
                    c["miscg"] += 32
                blk.gpsimd(f)
                MISC4 = c["miscg"]

                with nc.sbuf_tensor("cast_tmp", [16, H], BF16) as cast_tmp:
                    def f(g):
                        g.dma_start(out=cast_tmp[:, :], in_=bupl_d[:]).then_inc(s_f, 16)
                        g.wait_ge(s_ff, 16)
                        g.dma_start(out=cast_tmp[0:16, 0:D], in_=bdnl_d[:]).then_inc(s_f, 16)
                        g.wait_ge(s_ff, 32)
                        g.dma_start(out=cast_tmp[0:1, D:2 * D], in_=bdn_d[:]).then_inc(s_f, 16)
                        c["f"] += 48
                    blk.gpsimd(f)

                    def f(sy):
                        sy.wait_ge(s_f, 16)
                        sy.dma_start(out=bupl_b[:, :], in_=cast_tmp[:, :]).then_inc(s_ff, 16)
                        sy.wait_ge(s_f, 32)
                        sy.dma_start(out=bdnl_b[:, :], in_=cast_tmp[0:16, 0:D]).then_inc(s_ff, 16)
                        sy.wait_ge(s_f, 48)
                        sy.dma_start(out=bdn_b[:, :], in_=cast_tmp[0:1, D:2 * D]).then_inc(s_ff, 16)
                        c["ff"] += 48
                    blk.sync(f)
                F_CASTS = c["ff"]

                psU = e2.enter_context(nc.psum_tensor("psU", [128, 2, 512], F32))
                psu = e2.enter_context(nc.psum_tensor("psu", [16, 512], F32))
                psD = e2.enter_context(nc.psum_tensor("psD", [128, 4, 512], F32))
                psud = e2.enter_context(nc.psum_tensor("psud", [16, 512], F32))

                upevac = {}
                dnevac = {}
                st_ms = {}

                for hp in range(2):
                    def f(sy, hp=hp):
                        if hp == 0:
                            ph1wait(sy)
                        else:
                            sy.wait_ge(s_pe, ms["dn_done_0"])
                        sy.dma_start(out=xo_sh[:], in_=xn_in[512 * hp:512 * (hp + 1), :],
                                     transpose=True).then_inc(s_xo, 16)
                        cn["xo"] += 16
                    blk.sync(f)
                    XO_T = cn["xo"]
                    if stage == "fx":
                        def f(sy, XO_T=XO_T):
                            sy.wait_ge(s_xo, XO_T)
                            sy.wait_ge(s_cc, 10)
                        blk.sync(f)
                        break

                    def f(t, XO_T=XO_T, hp=hp):
                        t.wait_ge(s_xo, XO_T)
                        t.wait_ge(s_miscg, MISC4)
                        t.wait_ge(s_cc, CC_ALL)
                        if hp == 1:
                            t.wait_ge(s_ev, ms["ev_u_0"])
                        for dm in range(DMT):
                            inst = t.matmul(psu[:], aup2[:, dm, :], xo_sh[:, dm, :],
                                            start=(dm == 0), stop=(dm == DMT - 1))
                        inst.then_inc(s_pe, 1)
                        c["pe"] += 1
                    blk.tensor(f)
                    PE_U = c["pe"]

                    def f(v, PE_U=PE_U):
                        v.wait_ge(s_pe, PE_U)
                        v.tensor_copy(u_s[:], psu[:]).then_inc(s_ev, 1)
                        c["ev"] += 1
                    blk.vector(f)
                    EV_U = c["ev"]
                    ms[f"ev_u_{hp}"] = EV_U
                    if stage == "fuu":
                        def f(sy, EV_U=EV_U):
                            sy.wait_ge(s_ev, EV_U)
                        blk.sync(f)
                        break

                    for ht in range(HT):
                        par = ht % 2

                        def f(sy, hp=hp, ht=ht, par=par):
                            if f"wu_{par}" in st_ms:
                                sy.wait_ge(s_pe, st_ms[f"wu_{par}"])
                            sy.dma_start(out=wu2[par][:],
                                         in_=wup_all[128 * ht:128 * (ht + 1), :],
                                         transpose=True).then_inc(s_fu[par], 16)
                            cn["fu"][par] += 16
                            if ht < 2:
                                sy.wait_ge(s_ff, F_CASTS)
                            sy.dma_start(out=bupl2[par][:],
                                         in_=bupl_b[:, 128 * ht:128 * (ht + 1)]).then_inc(s_fu[par], 16)
                            cn["fu"][par] += 16
                        blk.sync(f)
                        FU_T = cn["fu"][par]

                        def f(t, ht=ht, par=par, FU_T=FU_T, XO_T=XO_T, EV_U=EV_U):
                            t.wait_ge(s_fu[par], FU_T)
                            t.wait_ge(s_ev, EV_U)
                            if f"up_{par}" in upevac:
                                t.wait_ge(s_ac, upevac[f"up_{par}"])
                            for dm in range(DMT):
                                t.matmul(psU[:, par, :], wu2[par][:, dm, :], xo_sh[:, dm, :],
                                         start=(dm == 0), stop=False)
                            inst = t.matmul(psU[:, par, :], bupl2[par][:], u_s[:],
                                            start=False, stop=True)
                            inst.then_inc(s_pe, 1)
                            c["pe"] += 1
                            st_ms[f"wu_{par}"] = c["pe"]
                        blk.tensor(f)
                        PE_UP = c["pe"]

                        def f(s, ht=ht, par=par, PE_UP=PE_UP, hp=hp):
                            s.wait_ge(s_pe, PE_UP)
                            if hp == 0 and ht == 0:
                                s.wait_ge(s_misc, MISC3)
                            s.activation(x2[:, ht, :], psU[:, par, :], AF.Relu,
                                         bias=bupt2[:, ht:ht + 1], scale=1.0).then_inc(s_ac, 1)
                            c["ac"] += 1
                            upevac[f"up_{par}"] = c["ac"]
                        blk.scalar(f)
                    ms[f"up_done_{hp}"] = c["pe"]
                    AC_X2 = c["ac"]
                    if stage == "fu":
                        def f(sy, AC_X2=AC_X2):
                            sy.wait_ge(s_ac, AC_X2)
                            sy.wait_ge(s_cc, c["cc"])
                        blk.sync(f)
                        break

                    ot_r = ot_all.rearrange("(b a p) t -> p (b a) t", p=128, a=DMT)

                    def f(sy, hp=hp):
                        sy.wait_ge(s_pe, ms[f"up_done_{hp}"])
                        sy.wait_ge(s_cc, CC_ALL)
                        pid = nc.partition_id(engines=[mybir.EngineType.SP])
                        bmy32 = (pid >> 1) * DMT
                        col0 = (pid & 1) * 1024 + 512 * hp
                        sy.dma_start(out=xo_sh[:],
                                     in_=ot_r[:, bass.ds(bmy32, DMT), bass.ds(col0, 512)]).then_inc(s_xo, 16)
                        cn["xo"] += 16
                    blk.sync(f)
                    XO2_T = cn["xo"]

                    def f(t, XO2_T=XO2_T, AC_X2=AC_X2, hp=hp):
                        t.wait_ge(s_ac, AC_X2)
                        if hp == 1:
                            t.wait_ge(s_ev, ms["ev_ud_0"])
                        for ht in range(HT):
                            inst = t.matmul(psud[:], adn2[:, ht, :], x2[:, ht, :],
                                            start=(ht == 0), stop=(ht == HT - 1))
                        inst.then_inc(s_pe, 1)
                        c["pe"] += 1
                    blk.tensor(f)
                    PE_UD = c["pe"]

                    def f(v, PE_UD=PE_UD):
                        v.wait_ge(s_pe, PE_UD)
                        v.tensor_copy(ud_s[:], psud[:]).then_inc(s_ev, 1)
                        c["ev"] += 1
                    blk.vector(f)
                    EV_UD = c["ev"]
                    ms[f"ev_ud_{hp}"] = EV_UD

                    NHG = (HT + 3) // 4
                    for dc in range(8):
                        par = dc % 2

                        def f(sy, hp=hp, dc=dc, par=par):
                            if f"wd_{par}" in st_ms:
                                sy.wait_ge(s_pe, st_ms[f"wd_{par}"])
                            pass
                        def f(sy, hp=hp, dc=dc, par=par):
                            if f"bdl_{par}" in st_ms:
                                sy.wait_ge(s_pe, st_ms[f"bdl_{par}"])
                            if dc < 2:
                                sy.wait_ge(s_ff, F_CASTS)
                            sy.dma_start(out=bdnl2[par][:],
                                         in_=bdnl_b[:, 512 * dc:512 * (dc + 1)]).then_inc(s_fb[par], 16)
                            cn["fb"][par] += 16
                            sy.dma_start(out=bdn2[par][:],
                                         in_=bdn_b[:, 512 * dc:512 * (dc + 1)]).then_inc(s_fb[par], 16)
                            cn["fb"][par] += 16
                        blk.sync(f)
                        FB_T = cn["fb"][par]

                        wo_T = []
                        for wog in range(4):
                            wpar = wog % 2

                            def f(g, hp=hp, dc=dc, wog=wog, wpar=wpar):
                                if f"wo_{wpar}" in st_ms:
                                    g.wait_ge(s_pe, st_ms[f"wo_{wpar}"])
                                g.dma_start(out=wo2[wpar][:],
                                            in_=wo_td[1024 * wog:1024 * (wog + 1),
                                                      512 * dc:512 * (dc + 1)]
                                            .rearrange("(a p) q -> p a q", p=128)).then_inc(s_fo[wpar], 16)
                                cn["fo"][wpar] += 16
                            blk.gpsimd(f)
                            wo_T.append(cn["fo"][wpar])

                            def f(t, hp=hp, dc=dc, wog=wog, wpar=wpar, WOT=wo_T[-1], XO2_T=XO2_T,
                                  EV_UD=EV_UD, FB_T=FB_T, par=par):
                                t.wait_ge(s_fo[wpar], WOT)
                                if wog == 0:
                                    t.wait_ge(s_xo, XO2_T)
                                    t.wait_ge(s_ev, EV_UD)
                                    t.wait_ge(s_fb[par], FB_T)
                                    for tt in range(4):
                                        if tt in dnevac:
                                            t.wait_ge(s_ev, dnevac[tt])
                                for tt in range(4):
                                    for ii in range(8):
                                        dvt = wog * 8 + ii
                                        inst = t.matmul(psD[:, tt, :],
                                                        xo_sh[:, dvt, 128 * tt:128 * (tt + 1)],
                                                        wo2[wpar][:, ii, :],
                                                        start=(wog == 0 and ii == 0), stop=False)
                                inst.then_inc(s_pe, 1)
                                c["pe"] += 1
                                st_ms[f"wo_{wpar}"] = c["pe"]
                            blk.tensor(f)

                        for gidx in range(NHG):
                            gpar = gidx % 2
                            h0 = gidx * 4
                            ghs = min(4, HT - h0)

                            def f(sy, hp=hp, dc=dc, gidx=gidx, gpar=gpar, h0=h0, ghs=ghs):
                                if f"wd_{gpar}" in st_ms:
                                    sy.wait_ge(s_pe, st_ms[f"wd_{gpar}"])
                                sy.dma_start(out=wd2[gpar][:, 0:ghs, :],
                                             in_=wdn_all[512 * dc:512 * (dc + 1),
                                                         128 * h0:128 * (h0 + ghs)],
                                             transpose=True).then_inc(s_fd[gpar], 16)
                                cn["fd"][gpar] += 16
                            blk.sync(f)
                            FD_T = cn["fd"][gpar]

                            def f(t, gidx=gidx, gpar=gpar, h0=h0, ghs=ghs, FD_T=FD_T):
                                t.wait_ge(s_fd[gpar], FD_T)
                                for tt in range(4):
                                    for ii in range(ghs):
                                        inst = t.matmul(psD[:, tt, :],
                                                        x2[:, h0 + ii, 128 * tt:128 * (tt + 1)],
                                                        wd2[gpar][:, ii, :],
                                                        start=False, stop=False)
                                inst.then_inc(s_pe, 1)
                                c["pe"] += 1
                                st_ms[f"wd_{gpar}"] = c["pe"]
                            blk.tensor(f)

                        def f(t, dc=dc, par=par):
                            for tt in range(4):
                                t.matmul(psD[:, tt, :], ud_s[:, 128 * tt:128 * (tt + 1)],
                                         bdnl2[par][:], start=False, stop=False)
                                inst = t.matmul(psD[:, tt, :], ones1[:, 0:128],
                                                bdn2[par][:], start=False, stop=True)
                                inst.then_inc(s_pe, 1)
                                c["pe"] += 1
                                st_ms[f"dn_tt{tt}"] = c["pe"]
                            st_ms[f"bdl_{par}"] = c["pe"]
                        blk.tensor(f)

                        if stage == "fd" and dc == 1:
                            def f(sy):
                                sy.wait_ge(s_outp[0], cn["outp"][0])
                                sy.wait_ge(s_outp[1], cn["outp"][1])
                            blk.sync(f)
                            break
                        for tt in range(4):
                            opar = (dc * 4 + tt + hp * 32) % 2
                            row0 = 512 * hp + 128 * tt

                            def f(sy, hp=hp, dc=dc, tt=tt, opar=opar, row0=row0):
                                if f"xr_{opar}" in st_ms:
                                    sy.wait_ge(s_ev, st_ms[f"xr_{opar}"])
                                sy.dma_start(out=xres[opar][:],
                                             in_=x_own[row0:row0 + 128,
                                                       512 * dc:512 * (dc + 1)]).then_inc(s_fx[opar], 16)
                                cn["fx"][opar] += 16
                            blk.sync(f)
                            FX_T = cn["fx"][opar]

                            def f(v, tt=tt, opar=opar, FX_T=FX_T, PED=st_ms[f"dn_tt{tt}"]):
                                v.wait_ge(s_pe, PED)
                                v.wait_ge(s_fx[opar], FX_T)
                                if cn["outp"][opar] > 0:
                                    v.wait_ge(s_outp[opar], cn["outp"][opar])
                                v.tensor_tensor(outb[opar][:], psD[:, tt, :], xres[opar][:],
                                                ALU.add).then_inc(s_ev, 1)
                                c["ev"] += 1
                                dnevac[tt] = c["ev"]
                                st_ms[f"xr_{opar}"] = c["ev"]
                            blk.vector(f)
                            EV_O = c["ev"]

                            def f(sy, hp=hp, dc=dc, tt=tt, opar=opar, row0=row0, EV_O=EV_O):
                                sy.wait_ge(s_ev, EV_O)
                                sy.dma_start(out=out_own[row0:row0 + 128, 512 * dc:512 * (dc + 1)],
                                             in_=outb[opar][:]).then_inc(s_outp[opar], 16)
                                cn["outp"][opar] += 16
                            blk.sync(f)
                    ms[f"dn_done_{hp}"] = c["pe"]
                    if stage == "fd":
                        break

                def f(sy):
                    sy.wait_ge(s_outp[0], cn["outp"][0])
                    sy.wait_ge(s_outp[1], cn["outp"][1])
                blk.sync(f)

    return nc


def run(stage, inputs):
    in_maps = host_prep(inputs)
    nc = build(stage)
    return run_bass_kernel_spmd(nc, in_maps, core_ids=list(range(NCORES)))


_NC_CACHE = {}


def kernel(**inputs):
    in_maps = host_prep(inputs)
    if "full" not in _NC_CACHE:
        _NC_CACHE["full"] = build("full")
    res = run_bass_kernel_spmd(_NC_CACHE["full"], in_maps, core_ids=list(range(NCORES)))
    out = np.concatenate([np.asarray(res.results[c]["out_own"], np.float32)
                          for c in range(NCORES)], 0)
    return out.reshape(B, S, D)


# version 06 (final)
# speedup vs baseline: 1.1046x; 1.1046x over previous
import os
import numpy as np
import ml_dtypes
import concourse.bass as bass
import concourse.mybir as mybir
from concourse.bass_utils import run_bass_kernel_spmd
from contextlib import ExitStack

F32 = mybir.dt.float32
BF16 = mybir.dt.bfloat16
I32 = mybir.dt.int32
AF = mybir.ActivationFunctionType
ALU = mybir.AluOpType
AX = mybir.AxisListType

NCORES = 8
B, S, D = 4, 2048, 4096
NT = B * S
TOWN = NT // NCORES
HS = 512
HD = 128
NHL = 4
H = 11008
QB = 256
NQB = S // QB
CB = 256
NCH = S // CB
SCALE = 1.0 / float(np.sqrt(128.0))
DMT = D // 128
HT = H // 128
UPR = H // NCORES
DNR = D // NCORES
DQF = 512


def nf4_terms():
    nf4 = np.array([-1.0, -0.6961928009986877, -0.5250730514526367,
                    -0.39491748809814453, -0.28444138169288635, -0.18477343022823334,
                    -0.09105003625154495, 0.0, 0.07958029955625534, 0.16093020141124725,
                    0.24611230194568634, 0.33791524171829224, 0.44070982933044434,
                    0.5626170039176941, 0.7229568362236023, 1.0], dtype=np.float64)
    alpha = nf4[0]
    beta = nf4[1] - nf4[0]
    terms = []
    for v in range(1, 15):
        g = (nf4[v + 1] - nf4[v]) - (nf4[v] - nf4[v - 1])
        if g >= 0:
            terms.append((g, -g * v, False))
        else:
            terms.append((g, -g * v, True))
            beta += g
            alpha += -g * v
    return float(alpha), float(beta), [(float(s), float(b), f) for s, b, f in terms]


def host_prep(inputs):
    x = np.asarray(inputs["L0_attn_x"])
    xr = np.ascontiguousarray(x.reshape(NT, D))
    wq_t = np.ascontiguousarray(np.asarray(inputs["L0_attn_Wq"]).T)
    wk_t = np.ascontiguousarray(np.asarray(inputs["L0_attn_Wk"]).T)
    wv_t = np.ascontiguousarray(np.asarray(inputs["L0_attn_Wv"]).T)
    wo_t = np.ascontiguousarray(np.asarray(inputs["L0_attn_Wo"]).T)

    inv = 1.0 / (10000.0 ** (np.arange(0, HD, 2, dtype=np.float64) / HD))
    t = np.arange(S, dtype=np.float64)
    fr = np.outer(t, inv)
    cos_t = np.concatenate([np.cos(fr)] * 2, axis=1).T
    sin_t = np.concatenate([np.sin(fr)] * 2, axis=1).T
    sin_eff = sin_t.copy()
    sin_eff[:64, :] *= -1.0
    cos_t = np.ascontiguousarray(cos_t.astype(np.float32))
    sin_eff = np.ascontiguousarray(sin_eff.astype(np.float32))

    kk = np.arange(128)[:, None]
    qq = np.arange(QB)[None, :]
    NEG = np.float32(-30000.0)
    m = np.zeros((128, 4, QB), np.float32)
    m[:, 0, :] = np.where(kk <= qq, 0.0, NEG)
    m[:, 1, :] = np.where(kk <= qq - 128, 0.0, NEG)
    m[:, 2, :] = np.where(kk > qq, 0.0, NEG)
    m[:, 3, :] = np.where(kk > qq - 128, 0.0, NEG)
    masks = np.ascontiguousarray(m.reshape(128, 4 * QB).astype(ml_dtypes.bfloat16))
    ident = np.ascontiguousarray(np.eye(128, dtype=ml_dtypes.bfloat16))

    _a, _b, terms = nf4_terms()
    consts = np.zeros((128, 16), np.float32)
    consts[:, 0] = 1e-5
    for i, (_s, bb, _f) in enumerate(terms):
        consts[:, 2 + i] = np.float32(bb)

    b_up = np.asarray(inputs["L0_ffn_b_up"]).astype(np.float32)
    bup_t = np.ascontiguousarray(b_up.reshape(HT, 128).T)
    bdn = np.ascontiguousarray(np.asarray(inputs["L0_ffn_b_down"]).astype(np.float32).reshape(1, D))
    aup = np.ascontiguousarray(np.asarray(inputs["L0_ffn_w_up_lora_a"]).astype(np.float32))
    bupl = np.ascontiguousarray(np.asarray(inputs["L0_ffn_w_up_lora_b"]).astype(np.float32))
    adn = np.ascontiguousarray(np.asarray(inputs["L0_ffn_w_down_lora_a"]).astype(np.float32))
    bdnl = np.ascontiguousarray(np.asarray(inputs["L0_ffn_w_down_lora_b"]).astype(np.float32))

    cu = np.asarray(inputs["L0_ffn_w_up_codes"]).reshape(H, D)
    au = np.asarray(inputs["L0_ffn_w_up_absmax"]).astype(np.float32).reshape(H, D // 64)
    cd = np.asarray(inputs["L0_ffn_w_down_codes"]).reshape(D, H)
    ad = np.asarray(inputs["L0_ffn_w_down_absmax"]).astype(np.float32).reshape(D, H // 64)

    in_maps = []
    for c in range(NCORES):
        in_maps.append(dict(
            x_own=np.ascontiguousarray(xr[c * TOWN:(c + 1) * TOWN]),
            wq_t=np.ascontiguousarray(wq_t[:, c * HS:(c + 1) * HS]),
            wk_t=np.ascontiguousarray(wk_t[:, c * HS:(c + 1) * HS]),
            wv_t=np.ascontiguousarray(wv_t[:, c * HS:(c + 1) * HS]),
            wo_t=wo_t,
            cos_t=cos_t, sin_eff=sin_eff, masks=masks, ident=ident, consts=consts,
            bup_t=bup_t, bdn=bdn, aup=aup, bupl=bupl, adn=adn, bdnl=bdnl,
            codes_up=np.ascontiguousarray(cu[c * UPR:(c + 1) * UPR]),
            am_up=np.ascontiguousarray(au[c * UPR:(c + 1) * UPR]),
            codes_dn=np.ascontiguousarray(cd[c * DNR:(c + 1) * DNR]),
            am_dn=np.ascontiguousarray(ad[c * DNR:(c + 1) * DNR]),
        ))
    return in_maps


def build(stage="full"):
    nbat = B
    if stage in ("qkv1", "attn1", "qp1", "qk1", "vmm1", "full1"):
        nbat = 1
        stage = stage[:-1]
    nc = bass.Bass("TRN2", target_bir_lowering=False, debug=False,
                   detect_race_conditions=not os.environ.get("KD_NORACE"))
    alpha, beta, terms = nf4_terms()

    x_own = nc.dram_tensor("x_own", [TOWN, D], F32, kind="ExternalInput").ap()
    wq_td = nc.dram_tensor("wq_t", [D, HS], F32, kind="ExternalInput").ap()
    wk_td = nc.dram_tensor("wk_t", [D, HS], F32, kind="ExternalInput").ap()
    wv_td = nc.dram_tensor("wv_t", [D, HS], F32, kind="ExternalInput").ap()
    wo_td = nc.dram_tensor("wo_t", [D, D], F32, kind="ExternalInput").ap()
    cos_td = nc.dram_tensor("cos_t", [HD, S], F32, kind="ExternalInput").ap()
    sin_td = nc.dram_tensor("sin_eff", [HD, S], F32, kind="ExternalInput").ap()
    masks_d = nc.dram_tensor("masks", [128, 4 * QB], BF16, kind="ExternalInput").ap()
    ident_d = nc.dram_tensor("ident", [128, 128], BF16, kind="ExternalInput").ap()
    consts_d = nc.dram_tensor("consts", [128, 16], F32, kind="ExternalInput").ap()
    bup_td = nc.dram_tensor("bup_t", [128, HT], F32, kind="ExternalInput").ap()
    bdn_d = nc.dram_tensor("bdn", [1, D], F32, kind="ExternalInput").ap()
    aup_d = nc.dram_tensor("aup", [D, 16], F32, kind="ExternalInput").ap()
    bupl_d = nc.dram_tensor("bupl", [16, H], F32, kind="ExternalInput").ap()
    adn_d = nc.dram_tensor("adn", [H, 16], F32, kind="ExternalInput").ap()
    bdnl_d = nc.dram_tensor("bdnl", [16, D], F32, kind="ExternalInput").ap()
    codes_up_d = nc.dram_tensor("codes_up", [UPR, D], I32, kind="ExternalInput").ap()
    am_up_d = nc.dram_tensor("am_up", [UPR, D // 64], F32, kind="ExternalInput").ap()
    codes_dn_d = nc.dram_tensor("codes_dn", [DNR, H], I32, kind="ExternalInput").ap()
    am_dn_d = nc.dram_tensor("am_dn", [DNR, H // 64], F32, kind="ExternalInput").ap()

    out_own = nc.dram_tensor("out_own", [TOWN, D], F32, kind="ExternalOutput").ap()

    xn_in = nc.dram_tensor("xn_in", [TOWN, D], BF16, kind="Internal").ap()
    xn_ag = [nc.dram_tensor(f"xn_ag{k}", [NCORES * 256, D], BF16, kind="Internal", addr_space="Shared").ap()
             for k in range(4)]
    o_in = nc.dram_tensor("o_in", [B * HS, S], BF16, kind="Internal").ap()
    ot_all = nc.dram_tensor("ot_all", [B * D, S], BF16, kind="Internal", addr_space="Shared").ap()
    wup_in = nc.dram_tensor("wup_in", [UPR, D], BF16, kind="Internal").ap()
    wup_all = nc.dram_tensor("wup_all", [H, D], BF16, kind="Internal", addr_space="Shared").ap()
    wdn_in = nc.dram_tensor("wdn_in", [DNR, H], BF16, kind="Internal").ap()
    wdn_all = nc.dram_tensor("wdn_all", [D, H], BF16, kind="Internal", addr_space="Shared").ap()
    bupl_b = nc.dram_tensor("bupl_b", [16, H], BF16, kind="Internal").ap()
    bdnl_b = nc.dram_tensor("bdnl_b", [16, D], BF16, kind="Internal").ap()
    bdn_b = nc.dram_tensor("bdn_b", [1, D], BF16, kind="Internal").ap()

    dbg = {}
    if stage in ("qkv", "vmm"):
        dbg["q"] = nc.dram_tensor("dbg_q", [128, NHL * S], BF16, kind="ExternalOutput").ap()
        dbg["k"] = nc.dram_tensor("dbg_k", [128, NHL * S], BF16, kind="ExternalOutput").ap()
        dbg["v"] = nc.dram_tensor("dbg_v", [128, (S // 128) * HS], BF16, kind="ExternalOutput").ap()
        dbg["xn"] = nc.dram_tensor("dbg_xn", [TOWN, D], BF16, kind="ExternalOutput").ap()
    if stage == "attn":
        dbg["o"] = nc.dram_tensor("dbg_o", [B * HS, S], BF16, kind="ExternalOutput").ap()
    if stage in ("qp", "qk"):
        dbg["q"] = nc.dram_tensor("dbg_q", [128, NHL * S], BF16, kind="ExternalOutput").ap()
        if stage == "qk":
            dbg["k"] = nc.dram_tensor("dbg_k", [128, NHL * S], BF16, kind="ExternalOutput").ap()
    if stage == "ln":
        dbg["xn"] = nc.dram_tensor("dbg_xn", [TOWN, D], BF16, kind="ExternalOutput").ap()
        dbg["ag"] = nc.dram_tensor("dbg_ag", [NCORES * 256, D], BF16, kind="ExternalOutput").ap()
    if stage == "deq":
        dbg["wup"] = nc.dram_tensor("dbg_wup", [UPR, D], BF16, kind="ExternalOutput").ap()
        dbg["wdn"] = nc.dram_tensor("dbg_wdn", [DNR, H], BF16, kind="ExternalOutput").ap()

    es = ExitStack()
    with es:
        SBp = lambda nm, shp, dt=F32: es.enter_context(nc.sbuf_tensor(nm, shp, dt))
        SEM = lambda nm: es.enter_context(nc.semaphore(nm))

        consts_s = SBp("consts_s", [128, 16], F32)
        ones1 = SBp("ones1", [1, 128], BF16)
        ones1f = SBp("ones1f", [1, 128], F32)
        ones128 = SBp("ones128", [128, 1], BF16)

        s_misc = SEM("s_misc")
        s_miscg = SEM("s_miscg")
        s_amp = [SEM("s_amp0"), SEM("s_amp1")]
        s_x = SEM("s_x")
        s_xfree = SEM("s_xfree")
        s_xnst = SEM("s_xnst")
        s_cc = SEM("s_cc")
        s_xnc = SEM("s_xnc")
        s_xncf = SEM("s_xncf")
        s_w = SEM("s_w")
        s_pe = SEM("s_pe")
        s_ev = SEM("s_ev")
        s_ac = SEM("s_ac")
        s_ost = SEM("s_ost")
        s_cod = SEM("s_cod")
        s_codp = [SEM("s_codp0"), SEM("s_codp1")]
        s_xncp = [SEM("s_xncp0"), SEM("s_xncp1")]
        s_wp = [SEM("s_wp0"), SEM("s_wp1")]
        s_ostp = [SEM("s_ostp0"), SEM("s_ostp1")]
        s_dqf = SEM("s_dqf")
        s_wst = SEM("s_wst")
        s_f = SEM("s_f")
        s_ff = SEM("s_ff")
        s_out = SEM("s_out")
        s_dbg = SEM("s_dbg")

        blk = es.enter_context(nc.Block())
        c = dict(misc=0, miscg=0, x=0, xfree=0, xnst=0, cc=0, xnc=0, xncf=0, w=0, w0=0, w1=0,
                 pe=0, ev=0, ac=0, ost=0, ost0=0, ost1=0, cod=0, dqf=0, wst=0, f=0, ff=0, out=0)
        ms = {}

        def f(sy):
            sy.dma_start(out=consts_s[:], in_=consts_d[:]).then_inc(s_misc, 16)
            c["misc"] += 16
        blk.sync(f)
        MISC_ALL = c["misc"]

        def f(v):
            v.memset(ones1[:], 1.0)
            v.memset(ones1f[:], 1.0)
            v.memset(ones128[:], 1.0).then_inc(s_ev, 1)
            c["ev"] += 1
        blk.vector(f)
        ms["ones"] = c["ev"]

        F_CASTS = 0

        e1 = ExitStack()
        with e1:
            SB1 = lambda nm, shp, dt=F32: e1.enter_context(nc.sbuf_tensor(nm, shp, dt))
            masks_s = SB1("masks_s", [128, 4, QB], BF16)
            ident_s = SB1("ident_s", [128, 128], BF16)
            sqn = SB1("sqn", [128, S], BF16)
            pt = SB1("pt", [128, 10, QB], BF16)
            qn2 = SB1("qn2", [1, S], BF16)
            bndA = SB1("bndA", [1, S], BF16)
            bndB = SB1("bndB", [1, S], BF16)
            nb = SB1("nb", [1, S], BF16)
            kmax = SB1("kmax", [1, 1], F32)
            rden = SB1("rden", [1, QB], F32)
            otmp = SB1("otmp", [128, QB], BF16)
            ostg = [SB1(f"ostg{i}", [128, QB], BF16) for i in range(2)]
            rtmp1 = SB1("rtmp1", [128, CB], F32)
            rtmp2 = SB1("rtmp2", [128, CB], F32)
            stt = {k: SB1(f"st_{k}", [128, 1], F32) for k in
                   ["mur", "mu", "varr", "vart", "mu2", "var", "ln", "rstd", "negmu"]}
            wbuf0 = SB1("wbuf0", [128, DMT, HS], BF16)

            eln = ExitStack()
            with eln:
                xt = eln.enter_context(nc.sbuf_tensor("xt", [128, D], F32))
                st_sq = eln.enter_context(nc.sbuf_tensor("st_sq", [128, D], BF16))
                xnout = eln.enter_context(nc.sbuf_tensor("xnout", [128, D], BF16))

                for i in range(TOWN // 128):
                    def f(sy, i=i):
                        if i > 0:
                            sy.wait_ge(s_xfree, i)
                        sy.dma_start(out=xt[:], in_=x_own[i * 128:(i + 1) * 128, :]).then_inc(s_x, 16)
                        c["x"] += 16
                    blk.sync(f)
                    X_T = c["x"]

                    def f(v, X_T=X_T, i=i):
                        v.wait_ge(s_misc, MISC_ALL)
                        v.wait_ge(s_x, X_T)
                        v.tensor_reduce(stt["mur"][:], xt[:], axis=AX.X, op=ALU.add).then_inc(s_ev, 1)
                        c["ev"] += 1
                        v.wait_ge(s_ev, c["ev"])
                        v.tensor_scalar(stt["mu"][:], stt["mur"][:], 1.0 / D, 0.0, ALU.mult, ALU.add).then_inc(s_ev, 1)
                        c["ev"] += 1
                    blk.vector(f)
                    EV_MU = c["ev"]

                    def f(s, EV_MU=EV_MU):
                        s.wait_ge(s_ev, EV_MU)
                        s.activation(st_sq[:], xt[:], AF.Square, bias=consts_s[:, 1:2], scale=1.0).then_inc(s_ac, 1)
                        c["ac"] += 1
                    blk.scalar(f)
                    AC_SQ = c["ac"]

                    def f(v, AC_SQ=AC_SQ):
                        v.wait_ge(s_ac, AC_SQ)
                        v.tensor_reduce(stt["varr"][:], st_sq[:], axis=AX.X, op=ALU.add).then_inc(s_ev, 1)
                        c["ev"] += 1
                        v.wait_ge(s_ev, c["ev"])
                        v.tensor_scalar(stt["vart"][:], stt["varr"][:], 1.0 / D, 0.0, ALU.mult, ALU.add).then_inc(s_ev, 1)
                        c["ev"] += 1
                        v.tensor_tensor(stt["mu2"][:], stt["mu"][:], stt["mu"][:], ALU.mult).then_inc(s_ev, 1)
                        c["ev"] += 1
                        v.wait_ge(s_ev, c["ev"])
                        v.tensor_tensor(stt["var"][:], stt["vart"][:], stt["mu2"][:], ALU.subtract).then_inc(s_ev, 1)
                        c["ev"] += 1
                    blk.vector(f)
                    EV_VAR = c["ev"]

                    def f(s, EV_VAR=EV_VAR):
                        s.wait_ge(s_ev, EV_VAR)
                        s.activation(stt["ln"][:], stt["var"][:], AF.Ln, bias=consts_s[:, 0:1], scale=1.0).then_inc(s_ac, 1)
                        c["ac"] += 1
                        s.wait_ge(s_ac, c["ac"])
                        s.activation(stt["rstd"][:], stt["ln"][:], AF.Exp, bias=consts_s[:, 1:2], scale=-0.5).then_inc(s_ac, 1)
                        c["ac"] += 1
                    blk.scalar(f)
                    AC_RSTD = c["ac"]

                    def f(v, AC_RSTD=AC_RSTD, i=i):
                        v.wait_ge(s_ac, AC_RSTD)
                        if i > 0:
                            v.wait_ge(s_xnst, 16 * i)
                        v.tensor_scalar(stt["negmu"][:], stt["mu"][:], -1.0, 0.0, ALU.mult, ALU.add).then_inc(s_ev, 1)
                        c["ev"] += 1
                        v.wait_ge(s_ev, c["ev"])
                        v.tensor_scalar(xnout[:], xt[:], stt["negmu"][:], stt["rstd"][:], ALU.add, ALU.mult).then_inc(s_ev, 1)
                        c["ev"] += 1
                        c["xfree"] += 1
                        v.nop().then_inc(s_xfree, 1)
                    blk.vector(f)
                    EV_XN = c["ev"]

                    def f(sy, i=i, EV_XN=EV_XN):
                        sy.wait_ge(s_ev, EV_XN)
                        sy.dma_start(out=xn_in[i * 128:(i + 1) * 128, :], in_=xnout[:]).then_inc(s_xnst, 16)
                        c["xnst"] += 16
                    blk.sync(f)
            LN_ALL = c["xnst"]

            vs = SB1("vs", [128, S // 128, HS], BF16)
            wbuf1 = SB1("wbuf1", [128, DMT, HS], BF16)
            wbuf = [wbuf0, wbuf1]
            xnb = [SB1(f"xnb{i}", [128, DMT, CB], BF16) for i in range(2)]
            qs = SB1("qs", [128, NHL, S], BF16)
            ks = SB1("ks", [128, NHL, S], BF16)
            cos_s = SB1("cos_s", [HD, S], F32)
            sin_s = SB1("sin_s", [HD, S], F32)
            cfb = [SB1(f"cf{i}", [128, DQF], F32) for i in range(2)]
            amb = [SB1(f"am{i}", [128, DQF // 64], F32) for i in range(2)]
            dq_tmp = [SB1(f"dqt{i}", [128, DQF], BF16) for i in range(4)]
            dq_acc = SB1("dq_acc", [128, DQF], F32)
            dq_w = SB1("dq_w", [128, DQF], BF16)

            def f(sy):
                sy.dma_start(out=cos_s[:], in_=cos_td[:]).then_inc(s_misc, 16)
                sy.dma_start(out=sin_s[:], in_=sin_td[:]).then_inc(s_misc, 16)
                sy.dma_start(out=masks_s[:], in_=masks_d.rearrange("p (a q) -> p a q", q=QB)).then_inc(s_misc, 16)
                sy.dma_start(out=ident_s[:], in_=ident_d[:]).then_inc(s_misc, 16)
                c["misc"] += 64
            blk.sync(f)
            MISC2 = c["misc"]

            if stage != "ln":
                def f(g):
                    g.dma_start(out=wbuf[0][:],
                                in_=wq_td.rearrange("(a p) q -> p a q", p=128)).then_inc(s_wp[0], 16)
                    c["w0"] += 16
                blk.gpsimd(f)
                ms["WQ0"] = (s_wp[0], c["w0"])

            def f(g):
                for k in range(4):
                    g.wait_ge(s_xnst, (2 * k + 2) * 16)
                    g.collective_compute("AllGather", ALU.bypass, replica_groups=[list(range(NCORES))],
                                         ins=[xn_in[256 * k:256 * (k + 1), :].opt()],
                                         outs=[xn_ag[k].opt()]).then_inc(s_cc, 1)
                    c["cc"] += 1
            blk.gpsimd(f)

            if stage != "ln":
                def f(g):
                    g.wait_ge(s_xnst, LN_ALL)
                    g.dma_start(out=wbuf[1][:],
                                in_=wk_td.rearrange("(a p) q -> p a q", p=128)).then_inc(s_wp[1], 16)
                    c["w1"] += 16
                blk.gpsimd(f)
                ms["WK0"] = (s_wp[1], c["w1"])

            def xn_src(t0):
                r = t0 // TOWN
                k = (t0 % TOWN) // 256
                row0 = 256 * r + (t0 % 256)
                return xn_ag[k][row0:row0 + CB, :]

            up_sched = []
            for r0 in range(0, UPR, 128):
                nr = min(128, UPR - r0)
                for col0 in range(0, D, DQF):
                    up_sched.append((codes_up_d, am_up_d, r0, nr, col0, DQF, wup_in))
            dn_sched = []
            for r0 in range(0, DNR, 128):
                for col0 in range(0, H, DQF):
                    ncol = min(DQF, H - col0)
                    dn_sched.append((codes_dn_d, am_dn_d, r0, 128, col0, ncol, wdn_in))
            dq_all = (up_sched + dn_sched) if stage in ("full", "deq", "p1", "fu", "fd", "fx", "fuu", "fz") else []
            N_UP = len(up_sched)
            N_DNC = (H + DQF - 1) // DQF
            N_DQ = len(dq_all)

            cc_items = []
            if stage in ("full", "deq", "p1", "fu", "fd", "fx", "fuu", "fz"):
                def mkup():
                    def f(g):
                        g.wait_ge(s_wst, 16 * N_UP)
                        g.collective_compute("AllGather", ALU.bypass,
                                             replica_groups=[list(range(NCORES))],
                                             ins=[wup_in[:, :].opt()],
                                             outs=[wup_all[:, :].opt()]).then_inc(s_cc, 1)
                        c["cc"] += 1
                    return f
                cc_items.append((N_UP, mkup()))
            if stage in ("attn", "full", "p1", "fu", "fd", "fx", "fuu", "fz"):
                for b in range(nbat):
                    def mk(b=b):
                        def f(g):
                            g.wait_ge(s_ostp[0], 16 * 16 * (b + 1))
                            g.wait_ge(s_ostp[1], 16 * 16 * (b + 1))
                            g.collective_compute("AllGather", ALU.bypass,
                                                 replica_groups=[list(range(NCORES))],
                                                 ins=[o_in[HS * b:HS * (b + 1), :].opt()],
                                                 outs=[ot_all[b * D:(b + 1) * D, :].opt()]).then_inc(s_cc, 1)
                            c["cc"] += 1
                        return f
                    cc_items.append((44 * (b + 1) if stage in ("full", "p1", "fu", "fd", "fx", "fuu", "fz") else 0, mk()))
            if stage in ("full", "deq", "p1", "fu", "fd", "fx", "fuu", "fz"):
                def mkdn():
                    def f(g):
                        g.wait_ge(s_wst, 16 * N_DQ)
                        g.collective_compute("AllGather", ALU.bypass,
                                             replica_groups=[list(range(NCORES))],
                                             ins=[wdn_in[:, :].opt()],
                                             outs=[wdn_all[:, :].opt()]).then_inc(s_cc, 1)
                        c["cc"] += 1
                    return f
                cc_items.append((N_DQ, mkdn()))
            cc_items.sort(key=lambda kv: kv[0])
            cc_ptr = [0]
            dq_state = dict(chunk=0)

            def flush_cc():
                while cc_ptr[0] < len(cc_items) and cc_items[cc_ptr[0]][0] <= dq_state["chunk"]:
                    blk.gpsimd(cc_items[cc_ptr[0]][1])
                    cc_ptr[0] += 1

            def emit_dequant_chunk(codes_src, am_src, r0, nr, col0, ncol, dst):
                ci = dq_state["chunk"]
                bi = ci % 2

                def f(g, bi=bi, ci=ci, r0=r0, nr=nr, col0=col0, ncol=ncol,
                      codes_src=codes_src):
                    if ci >= 2:
                        g.wait_ge(s_ac, ms[f"dqterm_{ci - 2}"])
                        g.wait_ge(s_ev, ms[f"dqbase_{ci - 2}"])
                    g.dma_start(out=cfb[bi][0:nr, 0:ncol],
                                in_=codes_src[r0:r0 + nr, col0:col0 + ncol]).then_inc(s_codp[bi], 16)
                    c["cod"] += 16
                blk.gpsimd(f)

                def f(sy, bi=bi, ci=ci, r0=r0, nr=nr, col0=col0, ncol=ncol, am_src=am_src):
                    if ci >= 2:
                        sy.wait_ge(s_ev, ms[f"dqw_{ci - 2}"])
                    sy.dma_start(out=amb[bi][0:nr, 0:ncol // 64],
                                 in_=am_src[r0:r0 + nr, col0 // 64:(col0 + ncol) // 64]).then_inc(s_amp[bi], 16)
                    c["cod"] += 16
                blk.sync(f)
                COD_T = (s_codp[bi], 16 * (ci // 2 + 1))
                AM_T = (s_amp[bi], 16 * (ci // 2 + 1))

                def f(v, bi=bi, COD_T=COD_T, AM_T=AM_T, nr=nr, ncol=ncol):
                    v.wait_ge(*COD_T)
                    v.wait_ge(*AM_T)
                    v.wait_ge(s_ev, c["ev"])
                    v.tensor_scalar(dq_acc[0:nr, 0:ncol], cfb[bi][0:nr, 0:ncol],
                                    beta, alpha, ALU.mult, ALU.add).then_inc(s_ev, 1)
                    c["ev"] += 1
                blk.vector(f)
                EV_BASE = c["ev"]
                ms[f"dqbase_{ci}"] = EV_BASE

                AC_BASE = c["ac"]
                tmp_war = tuple(ms.get(f"dqtmp{k}") for k in range(4))

                for lo, hi in ((0, 7), (7, len(terms))):
                    def f(s, COD_T=COD_T, AM_T=AM_T, bi=bi, nr=nr, ncol=ncol,
                          AC_BASE=AC_BASE, EV_BASE=EV_BASE, tmp_war=tmp_war,
                          lo=lo, hi=hi):
                        s.wait_ge(*COD_T)
                        s.wait_ge(*AM_T)
                        for ti in range(lo, hi):
                            sc_ = terms[ti][0]
                            tb = ti % 4
                            if ti < 4:
                                if tmp_war[tb] is not None:
                                    s.wait_ge(s_ev, tmp_war[tb])
                            else:
                                s.wait_ge(s_ev, EV_BASE + ti - 3)
                            s.activation(dq_tmp[tb][0:nr, 0:ncol], cfb[bi][0:nr, 0:ncol],
                                         AF.Relu, bias=consts_s[0:nr, 2 + ti:3 + ti],
                                         scale=sc_).then_inc(s_ac, 1)
                    blk.scalar(f)
                c["ac"] += len(terms)
                ms[f"dqterm_{ci}"] = c["ac"]

                def f(v, bi=bi, nr=nr, ncol=ncol, ci=ci, AC_BASE=AC_BASE, EV_BASE=EV_BASE):
                    for ti, (_sc, _bias, sub) in enumerate(terms):
                        v.wait_ge(s_ac, AC_BASE + ti + 1)
                        v.wait_ge(s_ev, EV_BASE + ti)
                        v.tensor_tensor(dq_acc[0:nr, 0:ncol], dq_acc[0:nr, 0:ncol],
                                        dq_tmp[ti % 4][0:nr, 0:ncol],
                                        ALU.subtract if sub else ALU.add).then_inc(s_ev, 1)
                    v.wait_ge(s_wst, 16 * ci)
                    v.wait_ge(s_ev, EV_BASE + len(terms))
                    insts = []
                    for b64 in range(ncol // 64):
                        insts.append(v.tensor_scalar(dq_w[0:nr, 64 * b64:64 * (b64 + 1)],
                                                     dq_acc[0:nr, 64 * b64:64 * (b64 + 1)],
                                                     amb[bi][0:nr, b64:b64 + 1], None, ALU.mult))
                    insts[-1].then_inc(s_ev, 1)
                blk.vector(f)
                c["ev"] += len(terms) + 1
                for k in range(4):
                    ti_last = max(t for t in range(len(terms)) if t % 4 == k)
                    ms[f"dqtmp{k}"] = EV_BASE + ti_last + 1
                EV_W = c["ev"]
                ms[f"dqw_{ci}"] = EV_W

                def f(sy, r0=r0, nr=nr, col0=col0, ncol=ncol, dst=dst, EV_W=EV_W):
                    sy.wait_ge(s_ev, EV_W)
                    sy.dma_start(out=dst[r0:r0 + nr, col0:col0 + ncol],
                                 in_=dq_w[0:nr, 0:ncol]).then_inc(s_wst, 16)
                    c["wst"] += 16
                blk.sync(f)
                dq_state["chunk"] += 1
                flush_cc()

            dq_ptr = [0]

            def pump_dequant(n):
                for _ in range(n):
                    if dq_ptr[0] < len(dq_all):
                        emit_dequant_chunk(*dq_all[dq_ptr[0]])
                        dq_ptr[0] += 1

            chunk_pe = {}
            if stage == "ln":
                def f(sy):
                    sy.wait_ge(s_cc, 4)
                    sy.wait_ge(s_xnst, c["xnst"])
                    sy.dma_start(out=dbg["xn"], in_=xn_in[:]).then_inc(s_dbg, 16)
                    sy.dma_start(out=dbg["ag"], in_=xn_ag[3][:, :]).then_inc(s_dbg, 16)
                    sy.wait_ge(s_dbg, 32)
                blk.sync(f)

            if stage != "ln":
                wsrc = {0: wq_td, 1: wk_td, 2: wv_td}

                def emit_wload(mat, slot, waits):
                    def f(g, mat=mat, slot=slot, waits=tuple(waits)):
                        for (sem, val) in waits:
                            g.wait_ge(sem, val)
                        g.dma_start(out=wbuf[slot][:],
                                    in_=wsrc[mat].rearrange("(a p) q -> p a q", p=128)).then_inc(s_wp[slot], 16)
                        c[f"w{slot}"] += 16
                    blk.gpsimd(f)
                    return (s_wp[slot], c[f"w{slot}"])

                for b in range(nbat):
                    epsum = ExitStack()
                    psA = epsum.enter_context(nc.psum_tensor(f"psA{b}", [128, 4, 512], F32))
                    psB = epsum.enter_context(nc.psum_tensor(f"psB{b}", [128, 4, 512], F32))
                    slotQ, slotK = b % 2, (b + 1) % 2
                    if b == 0:
                        WQ_T = ms["WQ0"]
                        WK_T = ms["WK0"]
                    else:
                        WQ_T = emit_wload(0, slotQ, [(s_pe, ms[f"vpass_{b-1}"])])
                        WK_T = emit_wload(1, slotK, [(s_pe, ms[f"kpass_{b-1}"])])

                    mats = (((WQ_T, slotQ, qs, "qpass"),) if stage == "qp" else
                            ((WQ_T, slotQ, qs, "qpass"), (WK_T, slotK, ks, "kpass")))
                    if stage == "qp":
                        pass
                    for mi, (W_T, slot, dst, mname) in enumerate(mats):
                        for cch in range(NCH):
                            gg = (b * 3 + mi) * NCH + cch
                            bi = gg % 2
                            ps = psA if cch % 2 == 0 else psB

                            def f(sy, b=b, cch=cch, gg=gg, bi=bi):
                                sy.wait_ge(s_cc, ((S * b + CB * cch) % TOWN) // 256 + 1)
                                if gg >= 2:
                                    sy.wait_ge(s_pe, chunk_pe[gg - 2])
                                sy.dma_start(out=xnb[bi][:],
                                             in_=xn_src(S * b + CB * cch),
                                             transpose=True).then_inc(s_xncp[bi], 16)
                                c["xnc"] += 16
                            blk.sync(f)
                            XNC_T = (s_xncp[bi], 16 * (gg // 2 + 1))

                            def f(t, W_T=W_T, XNC_T=XNC_T, slot=slot, bi=bi, ps=ps, cch=cch,
                                  aw=("exp" in ms and (ms["exp"], ms["otmp"], ms["rden"], ms["ofin"])) or None):
                                t.wait_ge(*W_T)
                                t.wait_ge(*XNC_T)
                                if aw is not None:
                                    t.wait_ge(s_ac, aw[0])
                                    t.wait_ge(s_ev, aw[1])
                                    t.wait_ge(s_ev, aw[2])
                                    t.wait_ge(s_ev, aw[3])
                                if "vevac_s0" in ms:
                                    t.wait_ge(s_ac, ms["vevac_s0"])
                                    t.wait_ge(s_ac, ms["vevac_s1"])
                                if f"rope_p{cch % 2}" in ms:
                                    t.wait_ge(s_ev, ms[f"rope_p{cch % 2}"])
                                for dq in range(4):
                                    for dm in range(DMT):
                                        inst = t.matmul(ps[:, dq, 0:QB], wbuf[slot][:, dm, dq * 128:(dq + 1) * 128],
                                                        xnb[bi][:, dm, :], start=(dm == 0),
                                                        stop=(dm == DMT - 1))
                                inst.then_inc(s_pe, 1)
                                c["pe"] += 1
                            blk.tensor(f)
                            PE_P = c["pe"]
                            chunk_pe[gg] = PE_P

                            def f(v, cch=cch, PE_P=PE_P, ps=ps, dst=dst):
                                v.wait_ge(s_pe, PE_P)
                                v.wait_ge(s_misc, MISC2)
                                tsl = slice(CB * cch, CB * (cch + 1))
                                for dq in range(4):
                                    v.wait_ge(s_ev, c["ev"])
                                    v.tensor_tensor(rtmp1[:], ps[:, dq, 0:QB], cos_s[:, tsl], ALU.mult)
                                    v.tensor_copy(rtmp2[0:64, :], ps[:, dq, 0:QB][64:128, :])
                                    v.tensor_copy(rtmp2[64:128, :], ps[:, dq, 0:QB][0:64, :]).then_inc(s_ev, 1)
                                    c["ev"] += 1
                                    v.wait_ge(s_ev, c["ev"])
                                    v.tensor_tensor(rtmp2[:], rtmp2[:], sin_s[:, tsl], ALU.mult).then_inc(s_ev, 1)
                                    c["ev"] += 1
                                    v.wait_ge(s_ev, c["ev"])
                                    v.tensor_tensor(dst[:, dq, tsl], rtmp1[:], rtmp2[:],
                                                    ALU.add).then_inc(s_ev, 1)
                                    c["ev"] += 1
                                ms[f"rope_p{cch % 2}"] = c["ev"]
                            blk.vector(f)
                        ms[f"{mname}_{b}"] = c["pe"]
                        if mname == "kpass" and stage not in ("qp", "qk"):
                            ms["WV_T"] = emit_wload(2, slotQ, [(s_pe, ms[f"qpass_{b}"])])
                        pump_dequant(4)

                    if stage in ("qp", "qk"):
                        def f(sy):
                            sy.wait_ge(s_ev, ms["rope_p0"])
                            sy.wait_ge(s_ev, ms["rope_p1"])
                            sy.dma_start(out=dbg["q"], in_=qs[:].rearrange("p a t -> p (a t)")).then_inc(s_dbg, 16)
                            if stage == "qk":
                                sy.dma_start(out=dbg["k"], in_=ks[:].rearrange("p a t -> p (a t)")).then_inc(s_dbg, 16)
                            sy.wait_ge(s_dbg, 32 if stage == "qk" else 16)
                        blk.sync(f)
                        epsum.close()
                        continue
                    WV_T = ms["WV_T"]
                    epsum.close()
                    epsum = ExitStack()
                    psV = epsum.enter_context(nc.psum_tensor(f"psV{b}", [128, 2, HS], F32))

                    for cch in range(NCH):
                        gg = (b * 3 + 2) * NCH + cch
                        bi = gg % 2

                        def f(sy, b=b, cch=cch, gg=gg, bi=bi):
                            sy.wait_ge(s_cc, ((S * b + CB * cch) % TOWN) // 256 + 1)
                            if gg >= 2:
                                sy.wait_ge(s_pe, chunk_pe[gg - 2])
                            sy.dma_start(out=xnb[bi][:],
                                         in_=xn_src(S * b + CB * cch),
                                         transpose=True).then_inc(s_xncp[bi], 16)
                            c["xnc"] += 16
                        blk.sync(f)
                        XNC_T = (s_xncp[bi], 16 * (gg // 2 + 1))

                        for ts_ in range(2):
                            def f(t, WV_T=WV_T, XNC_T=XNC_T, slotV=slotQ, bi=bi, ts_=ts_,
                                  rp=(ms.get("rope_p0"), ms.get("rope_p1")),
                                  aw=("exp" in ms and (ms["exp"], ms["otmp"], ms["rden"], ms["ofin"])) or None):
                                t.wait_ge(*WV_T)
                                t.wait_ge(*XNC_T)
                                if rp[0]:
                                    t.wait_ge(s_ev, rp[0])
                                if rp[1]:
                                    t.wait_ge(s_ev, rp[1])
                                if aw is not None:
                                    t.wait_ge(s_ac, aw[0])
                                    t.wait_ge(s_ev, aw[1])
                                    t.wait_ge(s_ev, aw[2])
                                    t.wait_ge(s_ev, aw[3])
                                if f"vevacv_s{ts_}" in ms:
                                    t.wait_ge(s_ev, ms[f"vevacv_s{ts_}"])
                                for dm in range(DMT):
                                    inst = t.matmul(psV[:, ts_, :],
                                                    xnb[bi][:, dm, ts_ * 128:(ts_ + 1) * 128],
                                                    wbuf[slotV][:, dm, :], start=(dm == 0), stop=(dm == DMT - 1))
                                inst.then_inc(s_pe, 1)
                                c["pe"] += 1
                                if ts_ == 1:
                                    chunk_pe[gg] = c["pe"]
                            blk.tensor(f)
                            PE_VS = c["pe"]

                            def f(s, cch=cch, ts_=ts_, PE_VS=PE_VS):
                                s.wait_ge(s_pe, PE_VS)
                                s.activation(vs[:, 2 * cch + ts_, :], psV[:, ts_, :], AF.Relu,
                                             bias=consts_s[:, 1:2], scale=1.0).then_inc(s_ac, 1) if False else None
                                s.nop().then_inc(s_ac, 1)
                                c["ac"] += 1
                                ms[f"vevac_s{ts_}"] = c["ac"]
                            blk.scalar(f)

                            def f(v, cch=cch, ts_=ts_, PE_VS=PE_VS):
                                v.wait_ge(s_pe, PE_VS)
                                if stage != "vmm":
                                    v.tensor_copy(vs[:, 2 * cch + ts_, :], psV[:, ts_, :]).then_inc(s_ev, 1)
                                else:
                                    v.nop().then_inc(s_ev, 1)
                                c["ev"] += 1
                                ms[f"vevacv_s{ts_}"] = c["ev"]
                            blk.vector(f)
                    ms[f"vpass_{b}"] = c["pe"]
                    epsum.close()
                    pump_dequant(4)

                    if stage in ("qkv", "deq"):
                        continue
                    for h in range(NHL):
                        with (
                            nc.psum_tensor(f"qn2p_{b}_{h}", [1, S], F32) as qn2p,
                            nc.psum_tensor(f"kn2p_{b}_{h}", [1, S], F32) as kn2p,
                        ):
                            def f(s, h=h):
                                s.wait_ge(s_ac, ms["vevac_s0"])
                                s.wait_ge(s_ac, ms["vevac_s1"])
                                s.wait_ge(s_ev, ms["rope_p0"])
                                s.wait_ge(s_ev, ms["rope_p1"])
                                if "pe_kn" in ms:
                                    s.wait_ge(s_pe, ms["pe_kn"])
                                s.activation(sqn[:], qs[:, h, :], AF.Square, bias=consts_s[:, 1:2],
                                             scale=1.0).then_inc(s_ac, 1)
                                c["ac"] += 1
                            blk.scalar(f)
                            AC_SQQ = c["ac"]

                            def f(t, AC_SQQ=AC_SQQ, qn2p=qn2p):
                                t.wait_ge(s_ac, AC_SQQ)
                                t.wait_ge(s_ev, ms["ones"])
                                if "exp" in ms:
                                    t.wait_ge(s_ac, ms["exp"])
                                    t.wait_ge(s_ev, ms["otmp"])
                                    t.wait_ge(s_ev, ms["rden"])
                                    t.wait_ge(s_ev, ms["ofin"])
                                for ch in range(4):
                                    inst = t.matmul(qn2p[0:1, 512 * ch:512 * (ch + 1)], ones128[:],
                                                    sqn[:, 512 * ch:512 * (ch + 1)], start=True, stop=True)
                                inst.then_inc(s_pe, 1)
                                c["pe"] += 1
                            blk.tensor(f)
                            PE_QN = c["pe"]

                            def f(s, h=h, PE_QN=PE_QN):
                                s.wait_ge(s_pe, PE_QN)
                                s.activation(sqn[:], ks[:, h, :], AF.Square, bias=consts_s[:, 1:2],
                                             scale=1.0).then_inc(s_ac, 1)
                                c["ac"] += 1
                            blk.scalar(f)
                            AC_SQK = c["ac"]

                            def f(t, AC_SQK=AC_SQK, kn2p=kn2p):
                                t.wait_ge(s_ac, AC_SQK)
                                for ch in range(4):
                                    inst = t.matmul(kn2p[0:1, 512 * ch:512 * (ch + 1)], ones128[:],
                                                    sqn[:, 512 * ch:512 * (ch + 1)], start=True, stop=True)
                                inst.then_inc(s_pe, 1)
                                c["pe"] += 1
                            blk.tensor(f)
                            PE_KN = c["pe"]
                            ms["pe_kn"] = PE_KN

                            def f(v, PE_KN=PE_KN, qn2p=qn2p, kn2p=kn2p):
                                v.wait_ge(s_pe, PE_KN)
                                v.tensor_reduce(kmax[:], kn2p[:], axis=AX.X, op=ALU.max).then_inc(s_ev, 1)
                                c["ev"] += 1
                                v.tensor_copy(qn2[:], qn2p[:]).then_inc(s_ev, 1)
                                c["ev"] += 1
                                v.wait_ge(s_ev, c["ev"])
                                v.tensor_scalar(bndA[0:1, :], qn2[0:1, :], kmax[0:1, 0:1], None,
                                                ALU.mult).then_inc(s_ev, 1)
                                c["ev"] += 1
                            blk.vector(f)
                            EV_BP = c["ev"]

                        def f(s, EV_BP=EV_BP):
                            s.wait_ge(s_ev, EV_BP)
                            s.activation(bndB[0:1, :], bndA[0:1, :], AF.Ln, bias=consts_s[0:1, 0:1],
                                         scale=1.0).then_inc(s_ac, 1)
                            c["ac"] += 1
                            s.wait_ge(s_ac, c["ac"])
                            s.activation(bndA[0:1, :], bndB[0:1, :], AF.Exp, bias=consts_s[0:1, 1:2],
                                         scale=0.5).then_inc(s_ac, 1)
                            c["ac"] += 1
                        blk.scalar(f)
                        AC_BND = c["ac"]

                        def f(v, AC_BND=AC_BND):
                            v.wait_ge(s_ac, AC_BND)
                            v.tensor_scalar(nb[0:1, :], bndA[0:1, :], -1.0, 0.0, ALU.mult,
                                            ALU.add).then_inc(s_ev, 1)
                            c["ev"] += 1
                        blk.vector(f)
                        EV_NB = c["ev"]

                        with (
                            nc.psum_tensor(f"sc_{b}_{h}", [128, 5, 512], F32) as sc,
                            nc.psum_tensor(f"ob_{b}_{h}", [128, QB], F32) as ob,
                            nc.psum_tensor(f"den_{b}_{h}", [1, QB], F32) as den,
                            nc.psum_tensor(f"bc_{b}_{h}", [128, QB], F32) as bc,
                        ):
                            for j in range(NQB):
                                kt = list(range(max(0, 2 * j - 8), 2 * j + 2))
                                nkt = len(kt)
                                waves = [kt[:5], kt[5:]]
                                PE0 = c["pe"]
                                AC0 = c["ac"]
                                nw = sum(1 for w in waves if w)

                                wn = 0
                                for wv, wkt in enumerate(waves):
                                    if not wkt:
                                        continue
                                    wn += 1

                                    def f(t, j=j, wv=wv, wkt=wkt, h=h, EV_NB=EV_NB, sc=sc,
                                          AC0=AC0):
                                        t.wait_ge(s_ev, EV_NB)
                                        t.wait_ge(s_misc, MISC2)
                                        if wv == 0:
                                            if f"exp_w" in ms:
                                                t.wait_ge(s_ac, ms["exp_w"])
                                            if "rden" in ms:
                                                t.wait_ge(s_ev, ms["rden"])
                                        else:
                                            t.wait_ge(s_ac, AC0 + 1)
                                        for idx, i in enumerate(wkt):
                                            t.matmul(sc[:, idx, 0:QB], ks[:, h, 128 * i:128 * (i + 1)],
                                                     qs[:, h, QB * j:QB * (j + 1)],
                                                     start=True, stop=False)
                                        for idx, i in enumerate(wkt):
                                            kind = None
                                            if i == 2 * j:
                                                kind = 0
                                            elif i == 2 * j + 1:
                                                kind = 1
                                            elif i == 2 * j - 8:
                                                kind = 2
                                            elif i == 2 * j - 7:
                                                kind = 3
                                            if kind is not None:
                                                t.matmul(sc[:, idx, 0:QB], ident_s[:],
                                                         masks_s[:, kind, :],
                                                         start=False, stop=False)
                                        for idx in range(len(wkt)):
                                            inst = t.matmul(sc[:, idx, 0:QB], ones1[:],
                                                            nb[0:1, QB * j:QB * (j + 1)],
                                                            start=False, stop=True)
                                        inst.then_inc(s_pe, 1)
                                        c["pe"] += 1
                                    blk.tensor(f)

                                def f(s, waves=waves, PE0=PE0, sc=sc):
                                    wn_ = 0
                                    for wv, wkt in enumerate(waves):
                                        if not wkt:
                                            continue
                                        wn_ += 1
                                        s.wait_ge(s_pe, PE0 + wn_)
                                        s.activation(pt[:, 5 * wv:5 * wv + len(wkt), :],
                                                     sc[:, 0:len(wkt), 0:QB], AF.Exp,
                                                     bias=consts_s[:, 1:2],
                                                     scale=SCALE).then_inc(s_ac, 1)
                                blk.scalar(f)
                                c["ac"] += nw
                                ms["exp_w"] = c["ac"]
                                ms["exp"] = c["ac"]
                                AC_EXP = c["ac"]

                                def f(t, kt=kt, nkt=nkt, h=h, AC_EXP=AC_EXP, ob=ob, den=den):
                                    t.wait_ge(s_ac, AC_EXP)
                                    if "otmp" in ms:
                                        t.wait_ge(s_ev, ms["otmp"])
                                    ptidx = [5 * (x // 5) + (x % 5) for x in range(nkt)]
                                    for n_, idx in enumerate(ptidx):
                                        t.matmul(den[0:1, :], ones128[:], pt[:, idx, :],
                                                 start=(n_ == 0), stop=(n_ == nkt - 1))
                                    for n_, i in enumerate(kt):
                                        inst = t.matmul(ob[:], vs[:, i, 128 * h:128 * (h + 1)],
                                                        pt[:, ptidx[n_], :],
                                                        start=(n_ == 0), stop=(n_ == nkt - 1))
                                    inst.then_inc(s_pe, 1)
                                    c["pe"] += 1
                                blk.tensor(f)
                                PE_AV = c["pe"]

                                def f(v, PE_AV=PE_AV, den=den):
                                    v.wait_ge(s_pe, PE_AV)
                                    v.wait_ge(s_ev, c["ev"])
                                    v.reciprocal(rden[0:1, :], den[0:1, :]).then_inc(s_ev, 1)
                                    c["ev"] += 1
                                    ms["rden"] = c["ev"]
                                blk.vector(f)
                                EV_RD = c["ev"]

                                def f(t, EV_RD=EV_RD, bc=bc):
                                    t.wait_ge(s_ev, EV_RD)
                                    t.matmul(bc[:], ones1f[:], rden[0:1, :], start=True,
                                             stop=True).then_inc(s_pe, 1)
                                    c["pe"] += 1
                                blk.tensor(f)
                                PE_BC = c["pe"]

                                def f(v, PE_BC=PE_BC, ob=ob):
                                    v.wait_ge(s_pe, PE_BC)
                                    v.wait_ge(s_ev, c["ev"])
                                    v.tensor_copy(otmp[:], ob[:]).then_inc(s_ev, 1)
                                    c["ev"] += 1
                                    ms["otmp"] = c["ev"]
                                blk.vector(f)
                                EV_OT = c["ev"]

                                ji = (h * NQB + j) % 2

                                def f(v, EV_OT=EV_OT, bc=bc, ji=ji,
                                      need=None):
                                    v.wait_ge(s_ev, EV_OT)
                                    if c[f"ost{ji}"] > 0:
                                        v.wait_ge(s_ostp[ji], c[f"ost{ji}"])
                                    v.tensor_tensor(ostg[ji][:], otmp[:], bc[:], ALU.mult).then_inc(s_ev, 1)
                                    c["ev"] += 1
                                    ms["ofin"] = c["ev"]
                                blk.vector(f)
                                EV_OF = c["ev"]

                                def f(sy, b=b, h=h, j=j, EV_OF=EV_OF, ji=ji):
                                    sy.wait_ge(s_ev, EV_OF)
                                    sy.dma_start(out=o_in[HS * b + 128 * h:HS * b + 128 * (h + 1),
                                                          QB * j:QB * (j + 1)],
                                                 in_=ostg[ji][:]).then_inc(s_ostp[ji], 16)
                                    c["ost"] += 16
                                    c[f"ost{ji}"] += 16
                                blk.sync(f)
                                pump_dequant(1)
                    ms[f"attn_{b}"] = c["pe"]

                pump_dequant(10000)
                flush_cc()
                while cc_ptr[0] < len(cc_items):
                    blk.gpsimd(cc_items[cc_ptr[0]][1])
                    cc_ptr[0] += 1

            WST_ALL = c["wst"]
            OST_ALL = c["ost"]
            CC_ALL = c["cc"]

            if stage in ("qkv", "vmm"):
                def f(sy):
                    sy.wait_ge(s_ev, ms["rope_p0"])
                    sy.wait_ge(s_ev, ms["rope_p1"])
                    sy.wait_ge(s_ac, ms["vevac_s0"])
                    sy.wait_ge(s_ac, ms["vevac_s1"])
                    sy.dma_start(out=dbg["q"], in_=qs[:].rearrange("p a t -> p (a t)")).then_inc(s_dbg, 16)
                    sy.dma_start(out=dbg["k"], in_=ks[:].rearrange("p a t -> p (a t)")).then_inc(s_dbg, 16)
                    sy.dma_start(out=dbg["v"], in_=vs[:].rearrange("p a t -> p (a t)")).then_inc(s_dbg, 16)
                    sy.wait_ge(s_xnst, c["xnst"])
                    sy.dma_start(out=dbg["xn"], in_=xn_in[:]).then_inc(s_dbg, 16)
                    sy.wait_ge(s_dbg, 64)
                blk.sync(f)
            if stage == "attn":
                def f(sy):
                    sy.wait_ge(s_ostp[0], c["ost0"])
                    sy.wait_ge(s_ostp[1], c["ost1"])
                    sy.dma_start(out=dbg["o"], in_=o_in[:]).then_inc(s_dbg, 16)
                    sy.wait_ge(s_dbg, 16)
                blk.sync(f)
            if stage in ("p1", "fz"):
                def f(sy):
                    sy.wait_ge(s_wst, WST_ALL)
                    sy.wait_ge(s_cc, c["cc"])
                    sy.wait_ge(s_ostp[0], c["ost0"])
                    sy.wait_ge(s_ostp[1], c["ost1"])
                blk.sync(f)
            if stage == "deq":
                def f(sy):
                    sy.wait_ge(s_wst, WST_ALL)
                    sy.dma_start(out=dbg["wup"], in_=wup_in[:]).then_inc(s_dbg, 16)
                    sy.dma_start(out=dbg["wdn"], in_=wdn_in[:]).then_inc(s_dbg, 16)
                    sy.wait_ge(s_dbg, 32)
                blk.sync(f)

        if stage in ("full", "fu", "fd", "fx", "fuu"):
            e2 = ExitStack()
            with e2:
                SB2 = lambda nm, shp, dt=BF16: e2.enter_context(nc.sbuf_tensor(nm, shp, dt))
                x2 = SB2("x2", [128, HT, 512])
                xo_sh = SB2("xo_sh", [128, DMT, 512])
                wu2 = [SB2(f"wu2_{i}", [128, DMT, 128]) for i in range(2)]
                wd2 = [SB2(f"wd2_{i}", [128, 4, 512]) for i in range(2)]
                wo2 = [SB2(f"wo2_{i}", [128, 8, 512]) for i in range(2)]
                aup2 = SB2("aup2", [128, DMT, 16])
                adn2 = SB2("adn2", [128, HT, 16])
                bupl2 = [SB2(f"bupl2_{i}", [16, 128]) for i in range(2)]
                bdnl2 = [SB2(f"bdnl2_{i}", [16, 512]) for i in range(2)]
                bdn2 = [SB2(f"bdn2_{i}", [1, 512]) for i in range(2)]
                bupt2 = e2.enter_context(nc.sbuf_tensor("bupt2", [128, HT], F32))
                u_s = SB2("u_s", [16, 512])
                ud_s = SB2("ud_s", [16, 512])
                xres = [e2.enter_context(nc.sbuf_tensor(f"xres{i}", [128, 512], F32)) for i in range(2)]
                outb = [e2.enter_context(nc.sbuf_tensor(f"outb{i}", [128, 512], F32)) for i in range(2)]

                s_fu = [SEM("s_fu0"), SEM("s_fu1")]
                s_fd = [SEM("s_fd0"), SEM("s_fd1")]
                s_fo = [SEM("s_fo0"), SEM("s_fo1")]
                s_fb = [SEM("s_fb0"), SEM("s_fb1")]
                s_fl = [SEM("s_fl0"), SEM("s_fl1")]
                s_fx = [SEM("s_fx0"), SEM("s_fx1")]
                s_xo = SEM("s_xo")
                s_outp = [SEM("s_outp0"), SEM("s_outp1")]
                cn = dict(fu=[0, 0], fd=[0, 0], fo=[0, 0], fb=[0, 0], fl=[0, 0], fx=[0, 0],
                          xo=0, outp=[0, 0])

                PH1 = [(s_ac, ms["exp"]), (s_ev, ms["otmp"]), (s_ev, ms["ofin"]),
                       (s_ev, ms["rden"]), (s_ev, ms["rope_p0"]), (s_ev, ms["rope_p1"]),
                       (s_ac, ms["vevac_s0"]), (s_ac, ms["vevac_s1"]),
                       (s_ev, ms[f"dqw_{N_DQ - 1}"] if N_DQ else 0),
                       (s_ac, ms[f"dqterm_{N_DQ - 1}"] if N_DQ else 0), (s_xnst, LN_ALL),
                       (s_ostp[0], c["ost0"]), (s_ostp[1], c["ost1"]),
                       (s_wst, c["wst"])]

                def ph1wait(eng):
                    for (sem, val) in PH1:
                        eng.wait_ge(sem, val)

                def f(sy):
                    ph1wait(sy)
                    sy.dma_start(out=bupt2[:], in_=bup_td[:]).then_inc(s_misc, 16)
                    c["misc"] += 16
                blk.sync(f)
                MISC3 = c["misc"]

                def f(g):
                    ph1wait(g)
                    g.dma_start(out=aup2[:], in_=aup_d.rearrange("(a p) r -> p a r", p=128)).then_inc(s_miscg, 16)
                    g.dma_start(out=adn2[:], in_=adn_d.rearrange("(a p) r -> p a r", p=128)).then_inc(s_miscg, 16)
                    c["miscg"] += 32
                blk.gpsimd(f)
                MISC4 = c["miscg"]

                with nc.sbuf_tensor("cast_tmp", [16, H], BF16) as cast_tmp:
                    def f(g):
                        g.dma_start(out=cast_tmp[:, :], in_=bupl_d[:]).then_inc(s_f, 16)
                        g.wait_ge(s_ff, 16)
                        g.dma_start(out=cast_tmp[0:16, 0:D], in_=bdnl_d[:]).then_inc(s_f, 16)
                        g.wait_ge(s_ff, 32)
                        g.dma_start(out=cast_tmp[0:1, D:2 * D], in_=bdn_d[:]).then_inc(s_f, 16)
                        c["f"] += 48
                    blk.gpsimd(f)

                    def f(sy):
                        sy.wait_ge(s_f, 16)
                        sy.dma_start(out=bupl_b[:, :], in_=cast_tmp[:, :]).then_inc(s_ff, 16)
                        sy.wait_ge(s_f, 32)
                        sy.dma_start(out=bdnl_b[:, :], in_=cast_tmp[0:16, 0:D]).then_inc(s_ff, 16)
                        sy.wait_ge(s_f, 48)
                        sy.dma_start(out=bdn_b[:, :], in_=cast_tmp[0:1, D:2 * D]).then_inc(s_ff, 16)
                        c["ff"] += 48
                    blk.sync(f)
                F_CASTS = c["ff"]

                psU = e2.enter_context(nc.psum_tensor("psU", [128, 2, 512], F32))
                psu = e2.enter_context(nc.psum_tensor("psu", [16, 512], F32))
                psD = e2.enter_context(nc.psum_tensor("psD", [128, 4, 512], F32))
                psud = e2.enter_context(nc.psum_tensor("psud", [16, 512], F32))

                upevac = {}
                dnevac = {}
                st_ms = {}

                for hp in range(2):
                    def f(sy, hp=hp):
                        if hp == 0:
                            ph1wait(sy)
                        else:
                            sy.wait_ge(s_pe, ms["dn_done_0"])
                        sy.dma_start(out=xo_sh[:], in_=xn_in[512 * hp:512 * (hp + 1), :],
                                     transpose=True).then_inc(s_xo, 16)
                        cn["xo"] += 16
                    blk.sync(f)
                    XO_T = cn["xo"]
                    if stage == "fx":
                        def f(sy, XO_T=XO_T):
                            sy.wait_ge(s_xo, XO_T)
                            sy.wait_ge(s_cc, 10)
                        blk.sync(f)
                        break

                    def f(t, XO_T=XO_T, hp=hp):
                        t.wait_ge(s_xo, XO_T)
                        t.wait_ge(s_miscg, MISC4)
                        t.wait_ge(s_cc, CC_ALL)
                        if hp == 1:
                            t.wait_ge(s_ev, ms["ev_u_0"])
                        for dm in range(DMT):
                            inst = t.matmul(psu[:], aup2[:, dm, :], xo_sh[:, dm, :],
                                            start=(dm == 0), stop=(dm == DMT - 1))
                        inst.then_inc(s_pe, 1)
                        c["pe"] += 1
                    blk.tensor(f)
                    PE_U = c["pe"]

                    def f(v, PE_U=PE_U):
                        v.wait_ge(s_pe, PE_U)
                        v.tensor_copy(u_s[:], psu[:]).then_inc(s_ev, 1)
                        c["ev"] += 1
                    blk.vector(f)
                    EV_U = c["ev"]
                    ms[f"ev_u_{hp}"] = EV_U
                    if stage == "fuu":
                        def f(sy, EV_U=EV_U):
                            sy.wait_ge(s_ev, EV_U)
                        blk.sync(f)
                        break

                    for ht in range(HT):
                        par = ht % 2

                        def f(sy, hp=hp, ht=ht, par=par):
                            if f"wu_{par}" in st_ms:
                                sy.wait_ge(s_pe, st_ms[f"wu_{par}"])
                            sy.dma_start(out=wu2[par][:],
                                         in_=wup_all[128 * ht:128 * (ht + 1), :],
                                         transpose=True).then_inc(s_fu[par], 16)
                            cn["fu"][par] += 16
                            if ht < 2:
                                sy.wait_ge(s_ff, F_CASTS)
                            sy.dma_start(out=bupl2[par][:],
                                         in_=bupl_b[:, 128 * ht:128 * (ht + 1)]).then_inc(s_fu[par], 16)
                            cn["fu"][par] += 16
                        blk.sync(f)
                        FU_T = cn["fu"][par]

                        def f(t, ht=ht, par=par, FU_T=FU_T, XO_T=XO_T, EV_U=EV_U):
                            t.wait_ge(s_fu[par], FU_T)
                            t.wait_ge(s_ev, EV_U)
                            if f"up_{par}" in upevac:
                                t.wait_ge(s_ac, upevac[f"up_{par}"])
                            for dm in range(DMT):
                                t.matmul(psU[:, par, :], wu2[par][:, dm, :], xo_sh[:, dm, :],
                                         start=(dm == 0), stop=False)
                            inst = t.matmul(psU[:, par, :], bupl2[par][:], u_s[:],
                                            start=False, stop=True)
                            inst.then_inc(s_pe, 1)
                            c["pe"] += 1
                            st_ms[f"wu_{par}"] = c["pe"]
                        blk.tensor(f)
                        PE_UP = c["pe"]

                        def f(s, ht=ht, par=par, PE_UP=PE_UP, hp=hp):
                            s.wait_ge(s_pe, PE_UP)
                            if hp == 0 and ht == 0:
                                s.wait_ge(s_misc, MISC3)
                            s.activation(x2[:, ht, :], psU[:, par, :], AF.Relu,
                                         bias=bupt2[:, ht:ht + 1], scale=1.0).then_inc(s_ac, 1)
                            c["ac"] += 1
                            upevac[f"up_{par}"] = c["ac"]
                        blk.scalar(f)
                    ms[f"up_done_{hp}"] = c["pe"]
                    AC_X2 = c["ac"]
                    if stage == "fu":
                        def f(sy, AC_X2=AC_X2):
                            sy.wait_ge(s_ac, AC_X2)
                            sy.wait_ge(s_cc, c["cc"])
                        blk.sync(f)
                        break

                    ot_r = ot_all.rearrange("(b a p) t -> p (b a) t", p=128, a=DMT)

                    def f(sy, hp=hp):
                        sy.wait_ge(s_pe, ms[f"up_done_{hp}"])
                        sy.wait_ge(s_cc, CC_ALL)
                        pid = nc.partition_id(engines=[mybir.EngineType.SP])
                        bmy32 = (pid >> 1) * DMT
                        col0 = (pid & 1) * 1024 + 512 * hp
                        sy.dma_start(out=xo_sh[:],
                                     in_=ot_r[:, bass.ds(bmy32, DMT), bass.ds(col0, 512)]).then_inc(s_xo, 16)
                        cn["xo"] += 16
                    blk.sync(f)
                    XO2_T = cn["xo"]

                    def f(t, XO2_T=XO2_T, AC_X2=AC_X2, hp=hp):
                        t.wait_ge(s_ac, AC_X2)
                        if hp == 1:
                            t.wait_ge(s_ev, ms["ev_ud_0"])
                        for ht in range(HT):
                            inst = t.matmul(psud[:], adn2[:, ht, :], x2[:, ht, :],
                                            start=(ht == 0), stop=(ht == HT - 1))
                        inst.then_inc(s_pe, 1)
                        c["pe"] += 1
                    blk.tensor(f)
                    PE_UD = c["pe"]

                    def f(v, PE_UD=PE_UD):
                        v.wait_ge(s_pe, PE_UD)
                        v.tensor_copy(ud_s[:], psud[:]).then_inc(s_ev, 1)
                        c["ev"] += 1
                    blk.vector(f)
                    EV_UD = c["ev"]
                    ms[f"ev_ud_{hp}"] = EV_UD

                    NHG = (HT + 3) // 4
                    for dc in range(8):
                        par = dc % 2

                        def f(sy, hp=hp, dc=dc, par=par):
                            if f"wd_{par}" in st_ms:
                                sy.wait_ge(s_pe, st_ms[f"wd_{par}"])
                            pass
                        def f(sy, hp=hp, dc=dc, par=par):
                            if f"bdl_{par}" in st_ms:
                                sy.wait_ge(s_pe, st_ms[f"bdl_{par}"])
                            if dc < 2:
                                sy.wait_ge(s_ff, F_CASTS)
                            sy.dma_start(out=bdnl2[par][:],
                                         in_=bdnl_b[:, 512 * dc:512 * (dc + 1)]).then_inc(s_fb[par], 16)
                            cn["fb"][par] += 16
                            sy.dma_start(out=bdn2[par][:],
                                         in_=bdn_b[:, 512 * dc:512 * (dc + 1)]).then_inc(s_fb[par], 16)
                            cn["fb"][par] += 16
                        blk.sync(f)
                        FB_T = cn["fb"][par]

                        wo_T = []
                        for wog in range(4):
                            wpar = wog % 2

                            def f(g, hp=hp, dc=dc, wog=wog, wpar=wpar):
                                if f"wo_{wpar}" in st_ms:
                                    g.wait_ge(s_pe, st_ms[f"wo_{wpar}"])
                                g.dma_start(out=wo2[wpar][:],
                                            in_=wo_td[1024 * wog:1024 * (wog + 1),
                                                      512 * dc:512 * (dc + 1)]
                                            .rearrange("(a p) q -> p a q", p=128)).then_inc(s_fo[wpar], 16)
                                cn["fo"][wpar] += 16
                            blk.gpsimd(f)
                            wo_T.append(cn["fo"][wpar])

                            def f(t, hp=hp, dc=dc, wog=wog, wpar=wpar, WOT=wo_T[-1], XO2_T=XO2_T,
                                  EV_UD=EV_UD, FB_T=FB_T, par=par):
                                t.wait_ge(s_fo[wpar], WOT)
                                if wog == 0:
                                    t.wait_ge(s_xo, XO2_T)
                                    t.wait_ge(s_ev, EV_UD)
                                    t.wait_ge(s_fb[par], FB_T)
                                    for tt in range(4):
                                        if tt in dnevac:
                                            t.wait_ge(s_ev, dnevac[tt])
                                for tt in range(4):
                                    for ii in range(8):
                                        dvt = wog * 8 + ii
                                        inst = t.matmul(psD[:, tt, :],
                                                        xo_sh[:, dvt, 128 * tt:128 * (tt + 1)],
                                                        wo2[wpar][:, ii, :],
                                                        start=(wog == 0 and ii == 0), stop=False)
                                inst.then_inc(s_pe, 1)
                                c["pe"] += 1
                                st_ms[f"wo_{wpar}"] = c["pe"]
                            blk.tensor(f)

                        for gidx in range(NHG):
                            gpar = gidx % 2
                            h0 = gidx * 4
                            ghs = min(4, HT - h0)

                            def f(sy, hp=hp, dc=dc, gidx=gidx, gpar=gpar, h0=h0, ghs=ghs):
                                if f"wd_{gpar}" in st_ms:
                                    sy.wait_ge(s_pe, st_ms[f"wd_{gpar}"])
                                sy.dma_start(out=wd2[gpar][:, 0:ghs, :],
                                             in_=wdn_all[512 * dc:512 * (dc + 1),
                                                         128 * h0:128 * (h0 + ghs)],
                                             transpose=True).then_inc(s_fd[gpar], 16)
                                cn["fd"][gpar] += 16
                            blk.sync(f)
                            FD_T = cn["fd"][gpar]

                            def f(t, gidx=gidx, gpar=gpar, h0=h0, ghs=ghs, FD_T=FD_T):
                                t.wait_ge(s_fd[gpar], FD_T)
                                for tt in range(4):
                                    for ii in range(ghs):
                                        inst = t.matmul(psD[:, tt, :],
                                                        x2[:, h0 + ii, 128 * tt:128 * (tt + 1)],
                                                        wd2[gpar][:, ii, :],
                                                        start=False, stop=False)
                                inst.then_inc(s_pe, 1)
                                c["pe"] += 1
                                st_ms[f"wd_{gpar}"] = c["pe"]
                            blk.tensor(f)

                        def f(t, dc=dc, par=par):
                            for tt in range(4):
                                t.matmul(psD[:, tt, :], ud_s[:, 128 * tt:128 * (tt + 1)],
                                         bdnl2[par][:], start=False, stop=False)
                                inst = t.matmul(psD[:, tt, :], ones1[:, 0:128],
                                                bdn2[par][:], start=False, stop=True)
                                inst.then_inc(s_pe, 1)
                                c["pe"] += 1
                                st_ms[f"dn_tt{tt}"] = c["pe"]
                            st_ms[f"bdl_{par}"] = c["pe"]
                        blk.tensor(f)

                        if stage == "fd" and dc == 1:
                            def f(sy):
                                sy.wait_ge(s_outp[0], cn["outp"][0])
                                sy.wait_ge(s_outp[1], cn["outp"][1])
                            blk.sync(f)
                            break
                        for tt in range(4):
                            opar = (dc * 4 + tt + hp * 32) % 2
                            row0 = 512 * hp + 128 * tt

                            def f(sy, hp=hp, dc=dc, tt=tt, opar=opar, row0=row0):
                                if f"xr_{opar}" in st_ms:
                                    sy.wait_ge(s_ev, st_ms[f"xr_{opar}"])
                                sy.dma_start(out=xres[opar][:],
                                             in_=x_own[row0:row0 + 128,
                                                       512 * dc:512 * (dc + 1)]).then_inc(s_fx[opar], 16)
                                cn["fx"][opar] += 16
                            blk.sync(f)
                            FX_T = cn["fx"][opar]

                            def f(v, tt=tt, opar=opar, FX_T=FX_T, PED=st_ms[f"dn_tt{tt}"]):
                                v.wait_ge(s_pe, PED)
                                v.wait_ge(s_fx[opar], FX_T)
                                if cn["outp"][opar] > 0:
                                    v.wait_ge(s_outp[opar], cn["outp"][opar])
                                v.tensor_tensor(outb[opar][:], psD[:, tt, :], xres[opar][:],
                                                ALU.add).then_inc(s_ev, 1)
                                c["ev"] += 1
                                dnevac[tt] = c["ev"]
                                st_ms[f"xr_{opar}"] = c["ev"]
                            blk.vector(f)
                            EV_O = c["ev"]

                            def f(sy, hp=hp, dc=dc, tt=tt, opar=opar, row0=row0, EV_O=EV_O):
                                sy.wait_ge(s_ev, EV_O)
                                sy.dma_start(out=out_own[row0:row0 + 128, 512 * dc:512 * (dc + 1)],
                                             in_=outb[opar][:]).then_inc(s_outp[opar], 16)
                                cn["outp"][opar] += 16
                            blk.sync(f)
                    ms[f"dn_done_{hp}"] = c["pe"]
                    if stage == "fd":
                        break

                def f(sy):
                    sy.wait_ge(s_outp[0], cn["outp"][0])
                    sy.wait_ge(s_outp[1], cn["outp"][1])
                blk.sync(f)

    return nc


def run(stage, inputs):
    in_maps = host_prep(inputs)
    nc = build(stage)
    return run_bass_kernel_spmd(nc, in_maps, core_ids=list(range(NCORES)))


_NC_CACHE = {}


def kernel(**inputs):
    in_maps = host_prep(inputs)
    if "full" not in _NC_CACHE:
        _NC_CACHE["full"] = build("full")
    res = run_bass_kernel_spmd(_NC_CACHE["full"], in_maps, core_ids=list(range(NCORES)))
    out = np.concatenate([np.asarray(res.results[c]["out_own"], np.float32)
                          for c in range(NCORES)], 0)
    return out.reshape(B, S, D)
